# version 36
# baseline (speedup 1.0000x reference)
import numpy as np
import concourse.bass as bass
import concourse.mybir as mybir
from concourse.bass_utils import run_bass_kernel_spmd

F32 = mybir.dt.float32
BF16 = mybir.dt.bfloat16
AF = mybir.ActivationFunctionType
ALU = mybir.AluOpType
AX = mybir.AxisListType

NCORES = 8
B = 2
T = 2048
D = 1024
H = 16
NT = 16
NG = 4
NE = 8
FF = 2048
BETA = 0.125
EPS = 1e-5
KB = 1024
SB_TOP = 229344
BASE0 = 16512


class Sched:
    ENGINES = ("pe", "act", "dve", "gp", "sync")

    def __init__(self):
        self.items = {e: [] for e in self.ENGINES}
        self.counts = {}

    def tok(self, sem):
        return (sem, self.counts.get(sem, 0))

    SERIALIZE = ("dve",)

    def emit(self, eng, fn, waits=(), inc=None, inc_by=1):
        w = [(sm, v) for t in waits if t is not None for (sm, v) in [t] if v > 0]
        if eng in self.SERIALIZE:
            w.append((eng, self.counts.get(eng, 0)))
            if inc is None:
                inc = eng
        tok = None
        if inc is not None:
            self.counts[inc] = self.counts.get(inc, 0) + inc_by
            tok = (inc, self.counts[inc])
        self.items[eng].append((fn, tuple(w), inc, inc_by))
        return tok

    def replay(self, blk, sems):
        def mkrun(name):
            def run(eng_obj):
                last = {}
                for fn, waits, inc, inc_by in self.items[name]:
                    for sm, v in waits:
                        if last.get(sm, 0) < v:
                            eng_obj.wait_ge(sems[sm], v)
                            last[sm] = v
                    ins = fn(eng_obj)
                    if inc is not None:
                        ins.then_inc(sems[inc], inc_by)
            return run
        if self.items["gp"]:
            blk.gpsimd(mkrun("gp"))
        if self.items["pe"]:
            blk.tensor(mkrun("pe"))
        if self.items["act"]:
            blk.scalar(mkrun("act"))
        if self.items["dve"]:
            blk.vector(mkrun("dve"))
        if self.items["sync"]:
            blk.sync(mkrun("sync"))


class Alloc:
    def __init__(self, nc):
        self.nc = nc
        self.cursors = {}

    def at(self, region_base, name, shape, dtype, cursor_key):
        cur = self.cursors.get(cursor_key, region_base)
        cur = (cur + 31) & ~31
        th = self.nc.alloc_sbuf_tensor_at(name, list(shape), dtype, offset=cur)
        per_part = int(np.prod(shape[1:])) * (2 if dtype == BF16 else 4)
        self.cursors[cursor_key] = cur + per_part
        assert self.cursors[cursor_key] <= SB_TOP, (
            f"SBUF overflow: {name} ends at {self.cursors[cursor_key]}")
        return th


def build_program(detect_races=False, dump_qkv=False, dump_ffn=False):
    nc = bass.Bass("TRN2", target_bir_lowering=False,
                   detect_race_conditions=detect_races)

    x_in = nc.dram_tensor("x_in", [B, T, D], F32, kind="ExternalInput").ap()
    wq = nc.dram_tensor("wq", [D, D], F32, kind="ExternalInput").ap()
    wk = nc.dram_tensor("wk", [D, D], F32, kind="ExternalInput").ap()
    wv = nc.dram_tensor("wv", [D, D], F32, kind="ExternalInput").ap()
    wo = nc.dram_tensor("wo", [D, D], F32, kind="ExternalInput").ap()
    bo = nc.dram_tensor("bo", [1, D], F32, kind="ExternalInput").ap()
    wsi = nc.dram_tensor("wsi", [D, FF], F32, kind="ExternalInput").ap()
    wso = nc.dram_tensor("wso", [D, D], F32, kind="ExternalInput").ap()
    w1 = nc.dram_tensor("w1", [D, FF], F32, kind="ExternalInput").ap()
    w2 = nc.dram_tensor("w2", [NE, D, D], F32, kind="ExternalInput").ap()
    wg = nc.dram_tensor("wg", [D, NE], F32, kind="ExternalInput").ap()
    gb = nc.dram_tensor("gb", [1, NE], F32, kind="ExternalInput").ap()
    out = nc.dram_tensor("out", [B, T, D], F32, kind="ExternalOutput").ap()
    if dump_qkv:
        dbgq = nc.dram_tensor("dbgq", [128, 8, 2048], F32, kind="ExternalOutput").ap()
        dbgk = nc.dram_tensor("dbgk", [128, 8, 2048], F32, kind="ExternalOutput").ap()
        dbgv = nc.dram_tensor("dbgv", [128, 16, 16, 66], F32, kind="ExternalOutput").ap()
        dbgo = nc.dram_tensor("dbgo", [64, 16, 512], F32, kind="ExternalOutput").ap()
        dbgx = nc.dram_tensor("dbgx", [128, 8, 512], F32, kind="ExternalOutput").ap()
        dbgmv = nc.dram_tensor("dbgmv", [128, 2, 2], F32, kind="ExternalOutput").ap()
        dbgrs = nc.dram_tensor("dbgrs", [128, 2, 1], F32, kind="ExternalOutput").ap()
    x2d = nc.dram_tensor("x2d", [B, T, D], F32, kind="ExternalOutput").ap()
    lscr = nc.dram_tensor("lscr", [2, 1, 2, 512], F32, kind="Internal").ap()
    if dump_ffn:
        dbgwd = nc.dram_tensor("dbgwd", [128, 16, 8], F32, kind="ExternalOutput").ap()
        dbght = nc.dram_tensor("dbght", [128, 8, 2048], F32, kind="ExternalOutput").ap()
        dbgrc = nc.dram_tensor("dbgrc", [128, 16, 1024], F32, kind="ExternalOutput").ap()
        dbghs = nc.dram_tensor("dbghs", [128, 8, 512], F32, kind="ExternalOutput").ap()
        dbg_gate = {}
        for nm in ("glt", "gex", "gkeep", "gt8"):
            dbg_gate[nm] = nc.dram_tensor("dbg_" + nm, [128, 2, 8], F32, kind="ExternalOutput").ap()
        for nm in ("nm1", "nm2", "gssum", "grs"):
            dbg_gate[nm] = nc.dram_tensor("dbg_" + nm, [128, 2, 1], F32, kind="ExternalOutput").ap()
        dbg_gsc = nc.dram_tensor("dbg_gsc", [8, 512], F32, kind="ExternalOutput").ap()

    al = Alloc(nc)
    s = Sched()
    tk = {}

    def last(*names):
        return [tk[n] for n in names if n in tk]

    P_SIZE = 35 * KB - 512
    P0 = SB_TOP - P_SIZE

    def P(name, shape, dtype):
        return al.at(P0, name, shape, dtype, "P")

    ident = P("ident", [128, 128], BF16)
    identf8 = P("identf8", [8, 8], F32)
    onesAB = P("onesAB", [128, 2], BF16)
    onescol = P("onescol", [128, 1], BF16)
    statg = P("statg", [1, 64], F32)
    ones1f = P("ones1f", [128, 128], F32)
    gb_bc = P("gb_bc", [128, NE], F32)
    qqm = P("qqm", [1, 16], F32)
    kkm = P("kkm", [1, 16], F32)
    tmp11 = P("tmp11", [1, 2], F32)
    bnout = P("bnout", [128, 4, 12], F32)
    mv = P("mv", [128, 4, 2], F32)
    rstd = P("rstd", [128, 4, 1], F32)
    tln = P("tln", [128, 4, 1], F32)
    nbv = P("nbv", [128, 4, 1], F32)
    bost = P("bost", [1, 1024], F32)
    gbst = P("gbst", [1, 8], F32)
    XG = P("XG", [128, 2, 1024], F32)
    XN = P("XN", [128, 2, 1024], BF16)
    YST = P("YST", [128, 2, 1024], F32)
    LNO = P("LNO", [128, 2, 1024], F32)
    glt4 = P("glt4", [128, 4, NE], F32)
    glt4p = P("glt4p", [128, 4, 16], BF16)
    gm14 = P("gm14", [128, 4, 1], F32)
    gex4 = P("gex4", [128, 4, NE], F32)
    gk4 = P("gk4", [128, 4, NE], F32)
    gs4 = P("gs4", [128, 4, 1], F32)
    grs4 = P("grs4", [128, 4, 1], F32)
    epsc = P("epsc", [128, 1], F32)
    nc.const_aps.register(F32, EPS, epsc[:])

    def RS(name, shape, dtype):
        return al.at(BASE0, name, shape, dtype, "AS")

    QT = RS("QT", [128, 8, 2048], BF16)
    KT = RS("KT", [128, 8, 2048], BF16)
    VT = RS("VT", [128, 16, 16, 66], BF16)
    LBT = RS("LBT", [128, 128], BF16)
    RJ = RS("RJ", [128, 4, 512], BF16)
    bo_bc = RS("bo_bc", [128, 1024], F32)
    AS_END = al.cursors["AS"]

    def R1(name, shape, dtype):
        return al.at(AS_END, name, shape, dtype, "A1")

    WQS = R1("WQS", [128, 8, 1024], BF16)
    WKS = R1("WKS", [128, 8, 1024], BF16)
    WVS = R1("WVS", [128, 8, 1024], BF16)
    XN1T = R1("XN1T", [128, 8, 512], BF16)
    sqb = R1("sqb", [128, 8, 512], BF16)
    XA = R1("XA", [128, 2, 1024], BF16)

    def R2(name, shape, dtype):
        return al.at(AS_END, name, shape, dtype, "A2")

    WOS = R2("WOS", [128, 8, 1024], BF16)
    OGT = R2("OGT", [128, 8, 512], BF16)
    OGTtmp = R2("OGTtmp", [128, 2, 512], BF16)
    PTB = R2("PTB", [128, 8, 512], BF16)
    LNV = R2("LNV", [128, 4, 512], F32)
    SLB = R2("SLB", [64, 4, 512], F32)
    MC = R2("MC", [128, 8, 1], F32)
    OGU = R2("OGU", [128, 6, 512], F32)

    assert al.cursors["A1"] <= P0 and al.cursors["A2"] <= P0

    def RF(name, shape, dtype):
        return al.at(BASE0, name, shape, dtype, "F")

    RACC = RF("RACC", [128, 8, 1024], F32)
    HT = RF("HT", [128, 8, 1024], BF16)
    WD = RF("WD", [128, 16, NE], F32)
    F_END = al.cursors["F"]

    def RF1(name, shape, dtype):
        return al.at(F_END, name, shape, dtype, "F1")

    XN2T = RF1("XN2T", [128, 8, 1024], BF16)
    WSH = RF1("WSH", [128, 8, 2048], BF16)
    W1S = RF1("W1S", [128, 8, 2048], BF16)
    WSOS = RF1("WSOS", [128, 8, 1024], BF16)
    WGS = RF1("WGS", [128, 8, NE], BF16)
    HST = RF1("HST", [128, 8, 512], BF16)
    SILA = RF1("SILA", [128, 2, 512], BF16)
    gsc8 = RF1("gsc8", [16, 512], BF16)
    W2S = RF1("W2S", [128, 2, 8, 512], BF16)

    assert al.cursors["F1"] <= P0, al.cursors["F1"]

    pSA = nc.alloc_psum_tensor("pSA", [128, 1024], F32)
    pSB = nc.alloc_psum_tensor("pSB", [128, 1024], F32)
    pS = [pSA[:, 0:512], pSA[:, 512:1024], pSB[:, 0:512], pSB[:, 512:1024]]
    pSpair = [pSA, pSB]
    pOr = [nc.alloc_psum_tensor("pO0", [128, 512], F32),
           nc.alloc_psum_tensor("pO1", [128, 512], F32)]
    pB = nc.alloc_psum_tensor("pB", [128, 512], F32)
    pW = nc.alloc_psum_tensor("pW", [128, 512], F32)
    pOrb = [pOr[0].bitcast(BF16), pOr[1].bitcast(BF16)]

    prev = [None]

    def gchain(fn):
        prev[0] = s.emit("gp", fn, waits=[prev[0]], inc="gp")
        return prev[0]

    gchain(lambda g: g.memset(ident[:], 0.0))
    gchain(lambda g: g.affine_select(out=ident[:], in_=ident[:],
                                     compare_op=ALU.not_equal, fill=1.0, base=0,
                                     pattern=[[-1, 128]], channel_multiplier=1))
    gchain(lambda g: g.memset(identf8[:], 0.0))
    gchain(lambda g: g.affine_select(out=identf8[:], in_=identf8[:],
                                     compare_op=ALU.not_equal, fill=1.0, base=0,
                                     pattern=[[-1, 8]], channel_multiplier=1))
    gchain(lambda g: g.memset(onesAB[:], 0.0))
    gchain(lambda g: g.memset(onesAB[0:64, 0:1], 1.0))
    gchain(lambda g: g.memset(onescol[:], 1.0))
    gchain(lambda g: g.memset(onesAB[64:128, 1:2], 1.0))
    gchain(lambda g: g.memset(ones1f[:], 1.0))
    gchain(lambda g: g.memset(LBT[:], 1.0))
    gchain(lambda g: g.affine_select(
        out=LBT[:], in_=LBT[:], compare_op=ALU.is_ge,
        fill=0.0, base=0, pattern=[[1, 128]], channel_multiplier=-1))
    gchain(lambda g: g.memset(RJ[:], 0.0))
    for j in range(4):
        gchain(lambda g, j=j: g.affine_select(
            out=RJ[:, j, :], in_=RJ[:, j, :], compare_op=ALU.not_equal,
            fill=-2048.0, base=-128 * j + 1, pattern=[[1, 512]],
            channel_multiplier=-1))
        if j > 0:
            gchain(lambda g, j=j: g.memset(RJ[0:1, j, 0:128 * j], -2048.0))
    gchain(lambda g: g.memset(epsc[:], EPS))
    gchain(lambda g: g.memset(VT[:, :, :, 64:66], 0.0))
    tk["init"] = gchain(lambda g: g.memset(VT[:, :, :, 64:65], 1.0))
    INIT = tk["init"]

    tk["bo_ld"] = s.emit("sync", lambda e: e.dma_start(out=bost[:], in_=bo[:]),
                         inc="dxa", inc_by=16)
    tk["gb_ld"] = s.emit("sync", lambda e: e.dma_start(out=gbst[:], in_=gb[:]),
                         inc="dxb", inc_by=16)
    for half in range(2):
        tkp = s.emit("pe", lambda t, half=half: t.matmul(
            pB[:, 0:512], ones1f[0:1, :], bost[0:1, half * 512:(half + 1) * 512],
            start=True, stop=True),
            waits=[tk["bo_ld"], INIT] + last("bo_bc0"), inc="pe")
        tk["bo_bc%d" % half] = s.emit(
            "dve", lambda v, half=half: v.tensor_copy(
                bo_bc[:, half * 512:(half + 1) * 512], pB[:, 0:512]),
            waits=[tkp], inc="dve")
    tkp = s.emit("pe", lambda t: t.matmul(pB[:, 0:8], ones1f[0:1, :], gbst[:],
                                          start=True, stop=True),
                 waits=[tk["gb_ld"], tk["bo_bc0"], tk["bo_bc1"]], inc="pe")
    tk["gb_bc"] = s.emit("dve", lambda v: v.tensor_copy(gb_bc[:], pB[:, 0:8]),
                         waits=[tkp], inc="dve")
    pB_free = tk["gb_bc"]

    def ln_pipeline(par, src_ap, dst_ap, waits, out_name, f32out=False):
        sp = str(par)
        t1 = s.emit("dve", lambda v: v.reduce_sum(mv[:, par, 0:1], src_ap, axis=AX.X),
                    waits=list(waits) + last("ln_mv_rd" + sp), inc="dve")
        tq = s.emit("act", lambda a: a.activation(dst_ap, src_ap, AF.Square,
                                                  accum_out=mv[:, par, 1:2]),
                    waits=list(waits) + last("ln_mv_rdA" + sp), inc="act")
        t2a = s.emit("dve", lambda v: v.tensor_tensor(
            tln[:, par, :], mv[:, par, 0:1], mv[:, par, 0:1], op=ALU.mult),
            waits=[tq], inc="dve")
        t2 = s.emit("dve", lambda v: v.scalar_tensor_tensor(
            bnout[:, par, 0:1], tln[:, par, :], -1.0 / D, mv[:, par, 1:2],
            ALU.mult, ALU.add), inc="dve")
        tk["ln_mv_rdA" + sp] = t2
        t3 = s.emit("act", lambda a: a.activation(tln[:, par, :], bnout[:, par, 0:1],
                                                  AF.Ln, bias=EPS, scale=1.0 / D),
                    waits=[t2, INIT] + last("ln_tln_rd" + sp), inc="act")
        t4 = s.emit("act", lambda a: a.activation(rstd[:, par, :], tln[:, par, :],
                                                  AF.Exp, bias=0.0, scale=-0.5),
                    waits=last("ln_rstd_rd" + sp), inc="act")
        tk["ln_tln_rd" + sp] = t4
        t5 = s.emit("dve", lambda v: v.scalar_tensor_tensor(
            nbv[:, par, :], mv[:, par, 0:1], -1.0 / D, rstd[:, par, :],
            ALU.mult, ALU.mult),
            waits=[t4] + last("ln_nb_rd" + sp), inc="dve")
        tk["ln_mv_rd" + sp] = t5
        tk["ln_rstd_rd" + sp] = t5
        if f32out:
            t6 = s.emit("dve", lambda v: v.tensor_scalar(
                dst_ap, src_ap, rstd[:, par, :], nbv[:, par, :], ALU.mult, ALU.add),
                waits=[t5], inc="dve")
        else:
            t6 = s.emit("act", lambda a: a.activation(
                dst_ap, src_ap, AF.Identity,
                bias=nbv[:, par, :], scale=rstd[:, par, :]),
                waits=[t5], inc="act")
        tk["ln_nb_rd" + sp] = t6
        tk[out_name] = t6
        return t6

    tp_par = [0]

    def transpose_tile(src_tile, dst_ap, waits, split=False):
        par = tp_par[0] % 2
        tp_par[0] += 1
        plan = ((("sync", "tp"), ("act", "tr")) if split else (("sync", "tp"),))
        nh = len(plan)
        for h, (eng, pre) in enumerate(plan):
            sem = "%s%d" % (pre, par)
            t = s.emit(eng, lambda e, h=h, nh=nh: e.dma_start_transpose(
                out=dst_ap[:, h * (8 // nh):(h + 1) * (8 // nh), :],
                in_=src_tile[:, h * (1024 // nh):(h + 1) * (1024 // nh)]),
                waits=list(waits) + [s.tok(sem)], inc=sem, inc_by=16)
            tk["%s_last%d" % (pre, par)] = t
        return last("tp_last0", "tp_last1", "tr_last0", "tr_last1")

    def pe_transpose_tile(src_tile, dst_ap, waits, aa):
        pbk = pOrb[aa]
        tpe = None
        for cc in range(8):
            tpe = s.emit("pe", lambda t, cc=cc, pbk=pbk: t.transpose(
                pbk[:, cc * 128:(cc + 1) * 128],
                src_tile[:, cc * 128:(cc + 1) * 128], ident[:]),
                waits=(list(waits) + ([pO_free[aa]] if pO_free[aa] else []))
                if cc == 0 else [],
                inc="pe" if cc == 7 else None)
        tev = s.emit("dve", lambda v, pbk=pbk: v.tensor_copy(
            dst_ap, pbk[:].rearrange("p (c t) -> p c t", c=8)),
            waits=[tpe], inc="dve")
        pO_free[aa] = tev
        return tev

    LNOb = LNO.bitcast(BF16)
    XNS = None
    XG_free = [tk["bo_ld"], tk["gb_ld"]]
    XN_free = [[], [], [], []]
    pS_free = [None, None, None, None]
    pW_free = None
    pO_free = [None, None]

    for b in range(B):
        wq_tok = s.emit("gp", lambda g: g.dma_start(
            out=WQS[:], in_=wq.rearrange("(cc p) d -> p cc d", p=128)),
            waits=last("A2_done"), inc="swq", inc_by=16)
        wk_tok = s.emit("gp", lambda g: g.dma_start(
            out=WKS[:], in_=wk.rearrange("(cc p) d -> p cc d", p=128)),
            waits=last("A2_done"), inc="swk", inc_by=16)
        wv_tok = s.emit("gp", lambda g: g.dma_start(
            out=WVS[:], in_=wv.rearrange("(cc p) d -> p cc d", p=128)),
            waits=last("A2_done"), inc="swv", inc_by=16)

        xn1t_ready = []
        stat_jobs = []
        ip = 0
        XNS = [XN[:, 0, :], XN[:, 1, :], LNOb[:, 0, 0:1024], LNOb[:, 1, 0:1024]]
        for g in range(NG):
            hoist2 = (b == 1 and g == 0)
            for tt in (range(2, 4) if hoist2 else range(4)):
                tile = 4 * g + tt
                j = tile % 2
                lt = s.emit("sync", lambda e, b=b, tile=tile, j=j: e.dma_start(
                    out=XG[:, j, :], in_=x_in[b, tile * 128:(tile + 1) * 128, :]),
                    waits=[XG_free[j]], inc="dxa" if j == 0 else "dxb", inc_by=16)
                ln_pipeline(j, XG[:, j, :], XNS[tt],
                            [lt] + XN_free[tt] +
                            (last("lno_rd%d" % j) if tt >= 2 else []),
                            "xn%d" % tt)
                XG_free[j] = tk["xn%d" % tt]
            for tt in range(4):
                wtok = ([tk["xnh%d" % (tt % 2)]] if (hoist2 and tt < 2)
                        else [tk["xn%d" % tt]])
                tev = pe_transpose_tile(XNS[tt],
                                        XN1T[:, :, tt * 128:(tt + 1) * 128],
                                        wtok + last("qkv_done"), tt % 2)
                XN_free[tt] = [tev]
                if tt >= 2:
                    tk["lno_rd%d" % (tt % 2)] = tev
            xn1t_ready = [tev]

            for wi, (Wsrc, wtok, DT) in enumerate(
                    ((WQS, wq_tok, QT), (WKS, wk_tok, KT))):
                for oc in range(8):
                    i = ip % 4
                    ip += 1
                    tmm = None
                    for cc in range(8):
                        tmm = s.emit("pe", lambda t, Wsrc=Wsrc, oc=oc, cc=cc, i=i: t.matmul(
                            pS[i], Wsrc[:, cc, oc * 128:(oc + 1) * 128],
                            XN1T[:, cc, :], start=(cc == 0), stop=(cc == 7)),
                            waits=(xn1t_ready + [wtok] +
                                   ([pS_free[i]] if pS_free[i] else [])) if cc == 0 else [],
                            inc="pe" if cc == 7 else None)
                    tev = s.emit("dve", lambda v, DT=DT, oc=oc, g=g, i=i: v.tensor_copy(
                        DT[:, oc, g * 512:(g + 1) * 512], pS[i]),
                        waits=[tmm], inc="dve")
                    pS_free[i] = tev
                    tsq = s.emit("gp", lambda p, DT=DT, oc=oc, g=g: p.tensor_tensor(
                        sqb[:, oc, :], DT[:, oc, g * 512:(g + 1) * 512],
                        DT[:, oc, g * 512:(g + 1) * 512], op=ALU.mult),
                        waits=[tev] + last("sqb_rd%d" % oc), inc="gp")
                    stat_jobs.append((wi, oc, g, tsq))

            tev = None
            vidx = 0
            for tt in range(4):
                kc = 4 * g + tt
                for dh in range(2):
                    i = ip % 4
                    ip += 1
                    tmm = None
                    for cc in range(8):
                        tmm = s.emit("pe", lambda t, tt=tt, dh=dh, cc=cc, i=i: t.matmul(
                            pS[i], XN1T[:, cc, tt * 128:(tt + 1) * 128],
                            WVS[:, cc, dh * 512:(dh + 1) * 512],
                            start=(cc == 0), stop=(cc == 7)),
                            waits=(xn1t_ready + [wv_tok] +
                                   ([pS_free[i]] if pS_free[i] else [])) if cc == 0 else [],
                            inc="pe" if cc == 7 else None)
                    tev = s.emit("dve", lambda v, kc=kc, dh=dh, i=i: v.tensor_copy(
                        VT[:, kc, dh * 8:(dh + 1) * 8, 0:64],
                        pS[i].rearrange("p (h d) -> p h d", d=64)),
                        waits=[tmm, INIT], inc="dve")
                    pS_free[i] = tev
                    for sidx in (2 * vidx, 2 * vidx + 1):
                        if sidx >= len(stat_jobs):
                            continue
                        wi, oc, gs, tsq = stat_jobs[sidx]
                        pbank = pB if sidx % 2 == 0 else pW
                        tpp = s.emit("pe", lambda t, oc=oc, pbank=pbank: t.matmul(
                            pbank[0:1, :], onescol[:], sqb[:, oc, :],
                            start=True, stop=True),
                            waits=[tsq, INIT, pB_free if sidx % 2 == 0 else pW_free],
                            inc="pe")
                        tk["sqb_rd%d" % oc] = tpp
                        col = wi * 32 + gs * 8 + oc
                        trd = s.emit("dve", lambda v, col=col, pbank=pbank: v.tensor_reduce(
                            statg[0:1, col:col + 1], pbank[0:1, :], AX.X, ALU.max),
                            waits=[tpp], inc="dve")
                        if sidx % 2 == 0:
                            pB_free = trd
                        else:
                            pW_free = trd
                    vidx += 1
            tk["qkv_done"] = tev
            stat_jobs = []

        a1_done = tk["qkv_done"]
        sg = statg.rearrange("a (w g c) -> a w c g", w=2, c=8)
        for ci in range(8):
            s.emit("dve", lambda v, ci=ci: v.tensor_reduce(
                qqm[0:1, ci:ci + 1], sg[0:1, 0, ci, :], AX.X, ALU.max),
                waits=[a1_done], inc="dve")
            s.emit("dve", lambda v, ci=ci: v.tensor_reduce(
                kkm[0:1, ci:ci + 1], sg[0:1, 1, ci, :], AX.X, ALU.max), inc="dve")
        tqa = s.emit("dve", lambda v: v.tensor_tensor(
            kkm[0:1, 8:16], qqm[0:1, 0:8], kkm[0:1, 0:8], op=ALU.add),
            waits=last("tmp11b_rd"), inc="dve")
        tb = s.emit("pe", lambda t: t.matmul(
            pB[:, 0:8], ones1f[0:1, :], kkm[0:1, 8:16], start=True, stop=True),
            waits=[tqa, pB_free], inc="pe")
        tk["tmp11b_rd"] = tb
        tmc = s.emit("act", lambda a: a.activation(
            MC[:, 0:8, 0], pB[:, 0:8], AF.Identity, bias=0.0, scale=-BETA / 2),
            waits=[tb] + last(*["mc_rd%d" % c for c in range(8)]), inc="act")
        for ci in range(8):
            tk["mc_%d" % ci] = tmc
        pB_free = tmc
        if dump_qkv and b == 0:
            s.emit("gp", lambda g: g.dma_start(out=dbgq[:], in_=QT[:]),
                   waits=[a1_done], inc="dbg", inc_by=16)
            s.emit("gp", lambda g: g.dma_start(out=dbgk[:], in_=KT[:]),
                   waits=[a1_done], inc="dbg", inc_by=16)
            s.emit("gp", lambda g: g.dma_start(out=dbgv[:], in_=VT[:]),
                   waits=[a1_done], inc="dbg", inc_by=16)
            s.emit("gp", lambda g: g.dma_start(out=dbgx[:], in_=XN1T[:]),
                   waits=[a1_done], inc="dbg", inc_by=16)
            s.emit("gp", lambda g: g.dma_start(out=dbgmv[:], in_=mv[:]),
                   waits=[a1_done], inc="dbg", inc_by=16)
            s.emit("gp", lambda g: g.dma_start(out=dbgrs[:], in_=rstd[:]),
                   waits=[a1_done], inc="dbg", inc_by=16)

        if b == 0:
            for tt in range(2):
                j = tt % 2
                lt = s.emit("gp", lambda e, tile=tt, j=j: e.dma_start(
                    out=XA[:, j, :], in_=x_in[1, tile * 128:(tile + 1) * 128, :]),
                    waits=[], inc="hxa" if j == 0 else "hxb", inc_by=16)
                ln_pipeline(2 + j, XA[:, j, :], XN[:, j, :],
                            [lt] + XN_free[j], "xnh%d" % j)

        wo_tok = s.emit("gp", lambda g: g.dma_start(
            out=WOS[:], in_=wo.rearrange("(cc p) c -> p cc c", p=128)),
            waits=[a1_done], inc="swo", inc_by=16)

        DEPTH = 2
        jobs = []
        for g in range(NG):
            for ci in range(8):
                nkc = 4 * g + 4
                for kc in range(nkc):
                    jobs.append((g, ci, kc, nkc))
        PT_tok = [None] * len(jobs)
        PT_free = [None, None, None, None]
        ogu_free = [None] * 6
        ognorm_last = None
        nj = len(jobs)
        pr_idx = [0]

        def emit_S(j):
            nonlocal pB_free
            g, ci, kc, nkc = jobs[j]
            i2 = j % 2
            i4 = j % 4
            diag = kc >= 4 * g
            jm = kc - 4 * g
            tS = None
            for aa in range(2):
                hb = aa * 64
                tS = s.emit("pe", lambda t, hb=hb, ci=ci, g=g, kc=kc, i2=i2, aa=aa, diag=diag: t.matmul(
                    pS[2 * i2 + aa], KT[hb:hb + 64, ci, kc * 128:(kc + 1) * 128],
                    QT[hb:hb + 64, ci, g * 512:(g + 1) * 512],
                    start=True, stop=not diag, tile_position=(hb, 0)),
                    waits=([a1_done] + ([pS_free[i2]] if pS_free[i2] else []))
                    if aa == 0 else [],
                    inc=None if diag else ("pe" if aa == 1 else None))
            if diag:
                for aa in range(2):
                    tS = s.emit("pe", lambda t, i2=i2, aa=aa, jm=jm: t.matmul(
                        pS[2 * i2 + aa], LBT[:], RJ[:, jm, :],
                        start=False, stop=True),
                        waits=[INIT] if aa == 0 else [],
                        inc="pe" if aa == 1 else None)
            tE = s.emit("act", lambda a, i2=i2, i4=i4, ci=ci: a.activation(
                PTB[:, 2 * i4:2 * i4 + 2, :], pSpair[i2][:], AF.Exp,
                bias=MC[:, ci, :], scale=BETA),
                waits=[tS, tk["mc_%d" % ci]] + ([PT_free[i4]] if PT_free[i4] else []),
                inc="act")
            pS_free[i2] = tE
            tk["mc_rd%d" % ci] = tE
            PT_tok[j] = tE

        norm_q = []
        wo_q = []

        def advance():
            nonlocal pB_free, pW_free, ognorm_last
            if wo_q:
                e = wo_q[0]
                if e["st"] == 0:
                    emit_wo_mm(e)
                else:
                    emit_wo_epi(e)
                    wo_q.pop(0)
                    return
                e["st"] += 1
                return
            if norm_q:
                e = norm_q[0]
                st = e["st"]
                g, ci, rot = e["g"], e["ci"], e["rot"]
                sl0, sl1 = 2 * (rot % 3), 2 * (rot % 3) + 1
                l0, l1 = 2 * (rot % 2), 2 * (rot % 2) + 1
                if st == 0:
                    t_ = s.emit("act", lambda a, sl0=sl0, l0=l0: a.activation(
                        LNV[64:65, l0:l0 + 2, :], OGU[64:65, sl0:sl0 + 2, :],
                        AF.Ln),
                        waits=e["ev"] + last("lnv_rd%d" % l0, "lnv_rd%d" % l1),
                        inc="act")
                    e["ln"] = s.emit("act", lambda a, l0=l0: a.activation(
                        LNV[64:65, l0:l0 + 2, :], LNV[64:65, l0:l0 + 2, :],
                        AF.Exp, bias=0.0, scale=-1.0),
                        waits=[t_], inc="act")
                elif st == 1:
                    p = rot % 2
                    t_ = s.emit("sync", lambda ee, p=p, l0=l0: ee.dma_start(
                        out=lscr[p], in_=LNV[64:65, l0:l0 + 2, :]),
                        waits=[e["ln"]] + last("lscr_rd%d" % p),
                        inc="lh%d" % p, inc_by=16)
                    tk["lnv_rd%d" % l0] = t_
                    tk["lnv_rd%d" % l1] = t_
                    tk["lh_last%d" % p] = t_
                    e["h1"] = t_
                elif st == 2:
                    p = rot % 2
                    t_ = s.emit("sync", lambda ee, p=p, l0=l0: ee.dma_start(
                        out=SLB[0:64, l0:l0 + 2, :],
                        in_=lscr[p].broadcast_to((64, 2, 512))),
                        waits=[e["h1"]] + last("slb_rd%d" % l0, "slb_rd%d" % l1),
                        inc="lb%d" % p, inc_by=16)
                    tk["lscr_rd%d" % p] = t_
                    tk["lb_last%d" % p] = t_
                    e["sx"] = [t_, t_]
                elif st == 3:
                    tN = s.emit("gp", lambda v, ci=ci, sl0=sl0, l0=l0: v.tensor_tensor(
                        OGT[0:64, ci, :], OGU[0:64, sl0, :], SLB[0:64, l0, :],
                        op=ALU.mult), waits=[e["sx"][0]] + last("ogt_rd"), inc="gp")
                    tk["slb_rd%d" % l0] = tN
                    ogu_free[sl0] = tN
                    pp = rot % 2
                    tN2 = s.emit("gp", lambda v, sl1=sl1, l1=l1, pp=pp: v.tensor_tensor(
                        OGTtmp[0:64, pp, :], OGU[0:64, sl1, :], SLB[0:64, l1, :],
                        op=ALU.mult),
                        waits=[e["sx"][1]] + last("ogtmp_rd%d" % pp, "ogt_rd"), inc="gp")
                    tk["slb_rd%d" % l1] = tN2
                    tN2 = s.emit("gp", lambda ee, ci=ci, pp=pp: ee.dma_start(
                        out=OGT[64:128, ci, :], in_=OGTtmp[0:64, pp, :]),
                        waits=[tN2], inc="som%d" % pp, inc_by=16)
                    tk["ogtmp_rd%d" % pp] = tN2
                    tk["som_last%d" % pp] = tN2
                    ogu_free[sl1] = tN2
                    ognorm_last = tN
                    norm_q.pop(0)
                    if ci == 7:
                        for qt in range(4):
                            wo_q.append({"g": g, "qt": qt, "st": 0})
                    return
                e["st"] += 1
                return
        pr_idx = [0]

        def emit_V(j):
            nonlocal pB_free, ognorm_last
            g, ci, kc, nkc = jobs[j]
            i4 = j % 4
            advance()
            tV = None
            for aa in range(2):
                tV = s.emit("pe", lambda t, ci=ci, kc=kc, i4=i4, nkc=nkc, aa=aa: t.matmul(
                    pOr[aa][0:65, :], VT[:, kc, 2 * ci + aa, 0:65],
                    PTB[:, 2 * i4 + aa, :],
                    start=(kc == 0), stop=(kc == nkc - 1)),
                    waits=[PT_tok[j]] +
                    ([pO_free[aa]] if (kc == 0 and pO_free[aa]) else []),
                    inc="pe")
            PT_free[i4] = tV
            if kc != nkc - 1:
                return
            rot = pr_idx[0]
            pr_idx[0] += 1
            ev = []
            for aa in range(2):
                sl = 2 * (rot % 3) + aa
                tEv = s.emit("dve", lambda v, aa=aa, sl=sl: v.tensor_copy(
                    OGU[0:65, sl, :], pOr[aa][0:65, :]),
                    waits=[tV] + ([ogu_free[sl]] if ogu_free[sl] else []), inc="dve")
                pO_free[aa] = tEv
                ev.append(tEv)
            norm_q.append({"g": g, "ci": ci, "rot": rot, "st": 0, "ev": ev})

        def emit_wo_mm(e):
            nonlocal pW_free, pB_free
            g, qt = e["g"], e["qt"]
            tile = g * 4 + qt
            j = tile % 2
            e["lt"] = s.emit("sync", lambda ee, b=b, tile=tile, j=j: ee.dma_start(
                out=XG[:, j, :], in_=x_in[b, tile * 128:(tile + 1) * 128, :]),
                waits=[XG_free[j]], inc="dxa" if j == 0 else "dxb", inc_by=16)
            ty = []
            for half in range(2):
                tmm = None
                for ih in range(8):
                    tmm = s.emit("pe", lambda t, ih=ih, qt=qt, half=half: t.matmul(
                        pW[:],
                        OGT[:, ih, qt * 128:(qt + 1) * 128],
                        WOS[:, ih, half * 512:(half + 1) * 512],
                        start=(ih == 0), stop=(ih == 7)),
                        waits=([ognorm_last, wo_tok] +
                               last("som_last0", "som_last1") +
                               ([pW_free] if pW_free else []))
                        if ih == 0 else [],
                        inc="pe" if ih == 7 else None)
                if qt == 3 and half == 1:
                    tk["ogt_rd"] = tmm
                tyh = s.emit("dve", lambda v, j=j, half=half: v.scalar_tensor_tensor(
                    YST[:, j, half * 512:(half + 1) * 512], pW[:], 1.0,
                    bo_bc[:, half * 512:(half + 1) * 512], ALU.mult, ALU.add),
                    waits=[tmm] + (last("yst_rd%d" % j) if half == 0 else []) +
                    [tk["bo_bc0"], tk["bo_bc1"]], inc="dve")
                pW_free = tyh
                ty.append(tyh)
            e["ty"] = ty

        def emit_wo_epi(e):
            nonlocal pB_free
            g, qt = e["g"], e["qt"]
            tile = g * 4 + qt
            j = tile % 2
            ln_pipeline(j, YST[:, j, :], LNO[:, j, :],
                        [e["ty"][1]] + last("lno_rd%d" % j), "ln_y", f32out=True)
            tx2 = s.emit("dve", lambda v, j=j: v.tensor_add(
                YST[:, j, :], LNO[:, j, :], XG[:, j, :]),
                waits=[tk["ln_y"], e["lt"]], inc="dve")
            tk["lno_rd%d" % j] = tx2
            XG_free[j] = tx2
            tst = s.emit("sync", lambda ee, b=b, tile=tile, j=j: ee.dma_start(
                out=x2d[b, tile * 128:(tile + 1) * 128, :], in_=YST[:, j, :]),
                waits=[tx2], inc="dsa" if j == 0 else "dsb", inc_by=16)
            tk["yst_rd%d" % j] = tst

        for j in range(nj):
            emit_S(j)
            if j >= DEPTH:
                emit_V(j - DEPTH)
        for j in range(nj - DEPTH, nj):
            emit_V(j)
        while norm_q or wo_q:
            advance()

        if dump_qkv and b == 0:
            s.emit("gp", lambda g: g.dma_start(out=dbgo[:], in_=OGT[:]),
                   waits=[ognorm_last], inc="dbg", inc_by=16)
        tk["A2_done"] = s.tok("dve")

    att_done = [s.tok("pe"), s.tok("act"), s.tok("dve"), s.tok("gp"),
                s.tok("dsa"), s.tok("dsb")]
    stores_done = [s.tok("dsa"), s.tok("dsb")]
    w2buf_free = [None, None]
    wg_tok = s.emit("gp", lambda g: g.dma_start(
        out=WGS[:], in_=wg.rearrange("(cc p) e -> p cc e", p=128)),
        waits=att_done, inc="swg", inc_by=16)
    wso_tok = s.emit("gp", lambda g: g.dma_start(
        out=WSOS[:], in_=wso.rearrange("(hc p) c -> p hc c", p=128)),
        waits=att_done, inc="swso", inc_by=16)
    wsi_tok = [s.emit("gp", lambda q: q.dma_start(
        out=WSH[:], in_=wsi.rearrange("(cc p) f -> p cc f", p=128)),
        waits=att_done, inc="swsh", inc_by=16)]
    w1_tok = [s.emit("gp", lambda q: q.dma_start(
        out=W1S[:], in_=w1.rearrange("(cc p) f -> p cc f", p=128)),
        waits=att_done, inc="swsh", inc_by=16)]
    tk["gsc8z"] = s.emit("gp", lambda g: g.memset(gsc8[:], 0.0),
                         waits=att_done, inc="gp")
    tk["gsc8_rd"] = tk["gsc8z"]

    def emit_tiles(b, hf):
        bar = att_done if b == 0 else []
        XNSF = [XN[:, 0, :], XN[:, 1, :],
                HST[:, 0:2, :].rearrange("p a b -> p (a b)"),
                HST[:, 2:4, :].rearrange("p a b -> p (a b)")]
        xn2rg = []
        for g0 in range(2):
            g = 2 * hf + g0
            for tt in range(4):
                tile = 4 * g + tt
                j = tile % 2
                lt = s.emit("sync", lambda e, b=b, tile=tile, j=j: e.dma_start(
                    out=XG[:, j, :], in_=x2d[b, tile * 128:(tile + 1) * 128, :]),
                    waits=[XG_free[j]] + (stores_done if tile == 0 else []),
                    inc="dxa" if j == 0 else "dxb", inc_by=16)
                ln_pipeline(2 + j, XG[:, j, :], XNSF[tt],
                            [lt] + XN_free[tt] +
                            (last("hst_rd") + (bar if tile < 8 else [])
                             if tt >= 2 else []), "xn2_%d" % tt)
                XG_free[j] = tk["xn2_%d" % tt]
            te = None
            for tt in range(4):
                tile = 4 * g + tt
                tl = tile - 8 * hf
                w_ = ([tk["xn2_%d" % tt]] + (bar if tile == 0 else []) +
                      (last("xn2t_rd") if tl == 0 else []))
                tev = pe_transpose_tile(
                    XNSF[tt], XN2T[:, :, tl * 128:(tl + 1) * 128],
                    w_, tt % 2)
                te = [tev]
                XN_free[tt] = te
                if tt >= 2:
                    tk["hst_rd"] = tev
            xn2rg.append(te)
        return xn2rg

    halves = [(bb, hh) for bb in range(B) for hh in range(2)]
    xn2pend = {halves[0]: emit_tiles(*halves[0])}
    ht_ready = None
    for idx, (b, hf) in enumerate(halves):
        bar = att_done if b == 0 else []
        xn2rg = xn2pend.pop((b, hf))

        def swiglu_group(wtok, wsrc, g0, g, dst_fn, war_name):
            th = None
            for fc in range(8):
                ia = sw_ip[0] % 4
                ib_ = (sw_ip[0] + 1) % 4
                sw_ip[0] += 2
                ta = None
                for cc in range(8):
                    ta = s.emit("pe", lambda t, fc=fc, cc=cc, g0=g0, ia=ia, wsrc=wsrc: t.matmul(
                        pS[ia], wsrc[:, cc, fc * 128:(fc + 1) * 128],
                        XN2T[:, cc, g0 * 512:(g0 + 1) * 512],
                        start=(cc == 0), stop=(cc == 7)),
                        waits=(wtok + xn2_ready + ([pS_free[ia]] if pS_free[ia] else []))
                        if cc == 0 else [],
                        inc="pe" if cc == 7 else None)
                tb = None
                for cc in range(8):
                    tb = s.emit("pe", lambda t, fc=fc, cc=cc, g0=g0, ib_=ib_, wsrc=wsrc: t.matmul(
                        pS[ib_], wsrc[:, cc, 1024 + fc * 128:1024 + (fc + 1) * 128],
                        XN2T[:, cc, g0 * 512:(g0 + 1) * 512],
                        start=(cc == 0), stop=(cc == 7)),
                        waits=([pS_free[ib_]] if pS_free[ib_] else [])
                        if cc == 0 else [],
                        inc="pe" if cc == 7 else None)
                si = fc % 2
                tsil = s.emit("act", lambda a, ia=ia, si=si: a.activation(
                    SILA[:, si, :], pS[ia], AF.Silu),
                    waits=[ta] + ([sila_free[si]] if sila_free[si] else []), inc="act")
                pS_free[ia] = tsil
                th = s.emit("dve", lambda v, fc=fc, g=g, si=si, ib_=ib_: v.tensor_tensor(
                    dst_fn(fc, g), SILA[:, si, :], pS[ib_], op=ALU.mult),
                    waits=[tb, tsil] + (last(war_name) if fc == 0 else []), inc="dve")
                sila_free[si] = th
                pS_free[ib_] = th
            return th

        sila_free = [None, None]
        sw_ip = [0]
        for g0 in range(2):
            g = 2 * hf + g0
            xn2_ready = xn2rg[g0]

            tgm = None
            for cc in range(8):
                tgm = s.emit("pe", lambda t, cc=cc, g0=g0: t.matmul(
                    pB[0:8, :], WGS[:, cc, :], XN2T[:, cc, g0 * 512:(g0 + 1) * 512],
                    start=(cc == 0), stop=(cc == 7)),
                    waits=(xn2_ready + [wg_tok, pB_free]) if cc == 0 else [],
                    inc="pe" if cc == 7 else None)
            tge = s.emit("dve", lambda v: v.tensor_copy(gsc8[0:8, :], pB[0:8, :]),
                         waits=[tgm] + last("gsc8_rd"), inc="dve")
            pB_free = tge
            pgx = g % 2
            tgx = s.emit("sync", lambda e: e.dma_start_transpose(
                out=glt4p[:], in_=gsc8[:]),
                waits=[tge] + last("glt4p_rd", "gsc8z") + [s.tok("gx%d" % pgx)],
                inc="gx%d" % pgx, inc_by=16)
            tk["gsc8_rd"] = tgx
            tk["gx_last%d" % pgx] = tgx
            tgl = s.emit("dve", lambda v: v.tensor_tensor(
                glt4[:], glt4p[:, :, 0:NE],
                gb_bc[:].rearrange("p (x e) -> p x e", x=1).broadcast_to([128, 4, NE]),
                op=ALU.add),
                waits=[tgx, tk["gb_bc"]] + last("glt_rd"), inc="dve")
            tk["glt4p_rd"] = tgl
            t1 = s.emit("dve", lambda v: v.reduce_max(
                gm14[:], glt4[:], axis=AX.X), waits=[tgl], inc="dve")
            tx = s.emit("act", lambda a: a.activation(gex4[:], glt4[:], AF.Exp),
                        waits=[tgl] + last("gex_rd"), inc="act")
            ts_ = s.emit("dve", lambda v: v.reduce_sum(
                gs4[:], gex4[:], axis=AX.X), waits=[tx], inc="dve")
            t3 = s.emit("dve", lambda v: v.tensor_tensor(
                gk4[:], glt4[:], gm14[:].broadcast_to([128, 4, 8]),
                op=ALU.is_ge), waits=[t1], inc="dve")
            t4 = s.emit("dve", lambda v: v.scalar_tensor_tensor(
                gk4[:], gk4[:], -1e30, glt4[:], ALU.mult, ALU.add), inc="dve")
            t5 = s.emit("dve", lambda v: v.reduce_max(
                gm14[:], gk4[:], axis=AX.X), inc="dve")
            t6 = s.emit("dve", lambda v: v.tensor_tensor(
                gk4[:], glt4[:], gm14[:].broadcast_to([128, 4, 8]), op=ALU.is_ge), inc="dve")
            tk["glt_rd"] = t6
            t7 = s.emit("dve", lambda v: v.tensor_tensor(
                gk4[:], gk4[:], gex4[:], op=ALU.mult), inc="dve")
            tk["gex_rd"] = t7
            t8 = s.emit("dve", lambda v: v.reciprocal(grs4[:], gs4[:]),
                        waits=[ts_], inc="dve")
            t9 = s.emit("dve", lambda v, g=g: v.tensor_tensor(
                WD[:, g * 4:(g + 1) * 4, :], gk4[:], grs4[:].broadcast_to([128, 4, 8]),
                op=ALU.mult), inc="dve")
            for qt in range(4):
                tk["wd_%d" % (g * 4 + qt)] = t9
            tk["gsc8_rd"] = t9

            th = swiglu_group(wsi_tok, WSH, g0, g,
                              lambda fc, g: HST[:, fc, :], "hst_rd")
            for qt in range(4):
                tile = g * 4 + qt
                tloc = tile - 8 * hf
                trc = None
                for half in range(2):
                    tmm = None
                    for hc in range(8):
                        tmm = s.emit("pe", lambda t, hc=hc, qt=qt, half=half: t.matmul(
                            pW[:],
                            HST[:, hc, qt * 128:(qt + 1) * 128],
                            WSOS[:, hc, half * 512:(half + 1) * 512],
                            start=(hc == 0), stop=(hc == 7)),
                            waits=([th, wso_tok] +
                                   ([pW_free] if pW_free else []))
                            if hc == 0 else [],
                            inc="pe" if hc == 7 else None)
                    trc = s.emit("dve", lambda v, tloc=tloc, half=half: v.tensor_copy(
                        RACC[:, tloc, half * 512:(half + 1) * 512], pW[:]),
                        waits=[tmm] + (bar if (tile == 0 and half == 0) else []),
                        inc="dve")
                    pW_free = trc
                tk["racc_%d" % tile] = trc
            tk["hst_rd"] = tmm

        for g0 in range(2):
            g = 2 * hf + g0
            xn2_ready = xn2rg[g0]
            ht_ready = swiglu_group(
                w1_tok, W1S, g0, g,
                lambda fc, g: HT[:, fc, (g % 2) * 512:((g % 2) + 1) * 512],
                "ht_rd")
        tk["xn2t_rd"] = s.tok("pe")

        if idx + 1 < len(halves):
            xn2pend[halves[idx + 1]] = emit_tiles(*halves[idx + 1])

        for e in range(NE):
            for ch in range(2):
                ib = (e * 2 + ch) % 2
                wt = s.emit("gp", lambda g, e=e, ch=ch, ib=ib: g.dma_start(
                    out=W2S[:, ib, :, :],
                    in_=w2[e].rearrange("(hc p) c -> p hc c", p=128)[:, :, ch * 512:(ch + 1) * 512]),
                    waits=[w2buf_free[ib]] if w2buf_free[ib] else att_done,
                    inc="sw2a" if ib == 0 else "sw2b", inc_by=16)
                for qt in range(8):
                    tile = hf * 8 + qt
                    i2 = qt % 4
                    tmm = None
                    for hc in range(8):
                        tmm = s.emit("pe", lambda t, hc=hc, qt=qt, ib=ib, i2=i2: t.matmul(
                            pS[i2], HT[:, hc, qt * 128:(qt + 1) * 128],
                            W2S[:, ib, hc, :], start=(hc == 0), stop=(hc == 7)),
                            waits=([wt, ht_ready] +
                                   ([pS_free[i2]] if pS_free[i2] else []))
                            if hc == 0 else [],
                            inc="pe" if hc == 7 else None)
                    tup = s.emit("dve",
                                 lambda v, e=e, qt=qt, tile=tile, ch=ch, i2=i2: v.scalar_tensor_tensor(
                                     RACC[:, qt, ch * 512:(ch + 1) * 512], pS[i2],
                                     WD[:, tile, e:e + 1],
                                     RACC[:, qt, ch * 512:(ch + 1) * 512],
                                     ALU.mult, ALU.add),
                                 waits=[tmm, tk["wd_%d" % tile], tk["racc_%d" % tile]],
                                 inc="dve")
                    tk["racc_%d" % tile] = tup
                    pS_free[i2] = tup
                    if e == NE - 1 and ch == 1:
                        j = tile % 2
                        lt = s.emit("gp", lambda ee, b=b, tile=tile, j=j: ee.dma_start(
                            out=YST[:, j, :], in_=x2d[b, tile * 128:(tile + 1) * 128, :]),
                            waits=last("yst_rd%d" % j), inc="fxa" if j == 0 else "fxb",
                            inc_by=16)
                        ln_pipeline(j, RACC[:, qt, :], LNO[:, j, :],
                                    [tup] + last("lno_rd%d" % j), "ln_f", f32out=True)
                        tof = s.emit("dve", lambda v, j=j: v.tensor_add(
                            YST[:, j, :], LNO[:, j, :], YST[:, j, :]),
                            waits=[tk["ln_f"], lt], inc="dve")
                        tk["lno_rd%d" % j] = tof
                        tst = s.emit("gp", lambda ee, b=b, tile=tile, j=j: ee.dma_start(
                            out=out[b, tile * 128:(tile + 1) * 128, :], in_=YST[:, j, :]),
                            waits=[tof], inc="fsa" if j == 0 else "fsb", inc_by=16)
                        tk["yst_rd%d" % j] = tst
                w2buf_free[ib] = tmm
        tk["ht_rd"] = tmm

    tk["f2_pe"] = s.tok("pe")
    tk["F3_done"] = s.tok("fsa")
    tk["F3_doneb"] = s.tok("fsb")

    s.emit("sync", lambda e: e.nop(),
           waits=[s.tok("dsa"), s.tok("dsb"), s.tok("dbg"),
                  s.tok("tp0"), s.tok("tp1"), s.tok("som0"), s.tok("som1"),
                  s.tok("lh0"), s.tok("lh1"), s.tok("lb0"), s.tok("lb1"),
                  s.tok("fxa"), s.tok("fxb"), s.tok("fsa"), s.tok("fsb"),
                  s.tok("gx0"), s.tok("gx1"), s.tok("hxa"), s.tok("hxb"),
                  s.tok("tr0"), s.tok("tr1"),
                  s.tok("swsh2"), s.tok("swsh3")])

    from contextlib import ExitStack
    with ExitStack() as ctx:
        sems = {}
        for name in ("pe", "act", "dve", "gp", "swq", "swk", "swv", "swo",
                     "swg", "swsh", "swsh2", "swsh3", "swso", "sw2a", "sw2b",
                     "dxa", "dxb", "dsa", "dsb", "dbg",
                     "tp0", "tp1", "som0", "som1",
                     "lh0", "lh1", "lb0", "lb1",
                     "fxa", "fxb", "fsa", "fsb", "gx0", "gx1",
                     "hxa", "hxb", "tr0", "tr1"):
            sems[name] = ctx.enter_context(nc.semaphore(name))
        blk = ctx.enter_context(nc.Block())
        s.replay(blk, sems)
    return nc


def kernel(L0_attn_x, L0_attn_Wq, L0_attn_Wk, L0_attn_Wv, L0_attn_Wo, L0_attn_bo,
           L0_ffn_w_shared_in, L0_ffn_w_shared_out, L0_ffn_w1_shared, L0_ffn_w2,
           L0_ffn_w_gate, L0_ffn_gate_bias):
    x = np.ascontiguousarray(np.asarray(L0_attn_x, dtype=np.float32))
    weights = dict(
        wq=np.ascontiguousarray(np.asarray(L0_attn_Wq, np.float32)),
        wk=np.ascontiguousarray(np.asarray(L0_attn_Wk, np.float32)),
        wv=np.ascontiguousarray(np.asarray(L0_attn_Wv, np.float32)),
        wo=np.ascontiguousarray(np.asarray(L0_attn_Wo, np.float32)),
        bo=np.ascontiguousarray(np.asarray(L0_attn_bo, np.float32).reshape(1, D)),
        wsi=np.ascontiguousarray(np.asarray(L0_ffn_w_shared_in, np.float32)),
        wso=np.ascontiguousarray(np.asarray(L0_ffn_w_shared_out, np.float32)),
        w1=np.ascontiguousarray(np.asarray(L0_ffn_w1_shared, np.float32)),
        w2=np.ascontiguousarray(np.asarray(L0_ffn_w2, np.float32)),
        wg=np.ascontiguousarray(np.asarray(L0_ffn_w_gate, np.float32)),
        gb=np.ascontiguousarray(np.asarray(L0_ffn_gate_bias, np.float32).reshape(1, NE)),
    )
    nc = build_program()
    in_maps = []
    for i in range(NCORES):
        m = dict(weights)
        m["x_in"] = np.ascontiguousarray(x[2 * i:2 * i + 2])
        in_maps.append(m)
    res = run_bass_kernel_spmd(nc, in_maps, core_ids=list(range(NCORES)))
    return np.concatenate([r["out"] for r in res.results], axis=0)


# version 38
# speedup vs baseline: 1.0059x; 1.0059x over previous
import numpy as np
import concourse.bass as bass
import concourse.mybir as mybir
from concourse.bass_utils import run_bass_kernel_spmd

F32 = mybir.dt.float32
BF16 = mybir.dt.bfloat16
AF = mybir.ActivationFunctionType
ALU = mybir.AluOpType
AX = mybir.AxisListType

NCORES = 8
B = 2
T = 2048
D = 1024
H = 16
NT = 16
NG = 4
NE = 8
FF = 2048
BETA = 0.125
EPS = 1e-5
KB = 1024
SB_TOP = 229344
BASE0 = 16512


class Sched:
    ENGINES = ("pe", "act", "dve", "gp", "sync")

    def __init__(self):
        self.items = {e: [] for e in self.ENGINES}
        self.counts = {}

    def tok(self, sem):
        return (sem, self.counts.get(sem, 0))

    SERIALIZE = ("dve",)

    def emit(self, eng, fn, waits=(), inc=None, inc_by=1):
        w = [(sm, v) for t in waits if t is not None for (sm, v) in [t] if v > 0]
        if eng in self.SERIALIZE:
            w.append((eng, self.counts.get(eng, 0)))
            if inc is None:
                inc = eng
        tok = None
        if inc is not None:
            self.counts[inc] = self.counts.get(inc, 0) + inc_by
            tok = (inc, self.counts[inc])
        self.items[eng].append((fn, tuple(w), inc, inc_by))
        return tok

    def replay(self, blk, sems):
        def mkrun(name):
            def run(eng_obj):
                last = {}
                for fn, waits, inc, inc_by in self.items[name]:
                    for sm, v in waits:
                        if last.get(sm, 0) < v:
                            eng_obj.wait_ge(sems[sm], v)
                            last[sm] = v
                    ins = fn(eng_obj)
                    if inc is not None:
                        ins.then_inc(sems[inc], inc_by)
            return run
        if self.items["gp"]:
            blk.gpsimd(mkrun("gp"))
        if self.items["pe"]:
            blk.tensor(mkrun("pe"))
        if self.items["act"]:
            blk.scalar(mkrun("act"))
        if self.items["dve"]:
            blk.vector(mkrun("dve"))
        if self.items["sync"]:
            blk.sync(mkrun("sync"))


class Alloc:
    def __init__(self, nc):
        self.nc = nc
        self.cursors = {}

    def at(self, region_base, name, shape, dtype, cursor_key):
        cur = self.cursors.get(cursor_key, region_base)
        cur = (cur + 31) & ~31
        th = self.nc.alloc_sbuf_tensor_at(name, list(shape), dtype, offset=cur)
        per_part = int(np.prod(shape[1:])) * (2 if dtype == BF16 else 4)
        self.cursors[cursor_key] = cur + per_part
        assert self.cursors[cursor_key] <= SB_TOP, (
            f"SBUF overflow: {name} ends at {self.cursors[cursor_key]}")
        return th


def build_program(detect_races=False, dump_qkv=False, dump_ffn=False):
    nc = bass.Bass("TRN2", target_bir_lowering=False,
                   detect_race_conditions=detect_races)

    x_in = nc.dram_tensor("x_in", [B, T, D], F32, kind="ExternalInput").ap()
    wq = nc.dram_tensor("wq", [D, D], F32, kind="ExternalInput").ap()
    wk = nc.dram_tensor("wk", [D, D], F32, kind="ExternalInput").ap()
    wv = nc.dram_tensor("wv", [D, D], F32, kind="ExternalInput").ap()
    wo = nc.dram_tensor("wo", [D, D], F32, kind="ExternalInput").ap()
    bo = nc.dram_tensor("bo", [1, D], F32, kind="ExternalInput").ap()
    wsi = nc.dram_tensor("wsi", [D, FF], F32, kind="ExternalInput").ap()
    wso = nc.dram_tensor("wso", [D, D], F32, kind="ExternalInput").ap()
    w1 = nc.dram_tensor("w1", [D, FF], F32, kind="ExternalInput").ap()
    w2 = nc.dram_tensor("w2", [NE, D, D], F32, kind="ExternalInput").ap()
    wg = nc.dram_tensor("wg", [D, NE], F32, kind="ExternalInput").ap()
    gb = nc.dram_tensor("gb", [1, NE], F32, kind="ExternalInput").ap()
    out = nc.dram_tensor("out", [B, T, D], F32, kind="ExternalOutput").ap()
    if dump_qkv:
        dbgq = nc.dram_tensor("dbgq", [128, 8, 2048], F32, kind="ExternalOutput").ap()
        dbgk = nc.dram_tensor("dbgk", [128, 8, 2048], F32, kind="ExternalOutput").ap()
        dbgv = nc.dram_tensor("dbgv", [128, 16, 16, 66], F32, kind="ExternalOutput").ap()
        dbgo = nc.dram_tensor("dbgo", [64, 16, 512], F32, kind="ExternalOutput").ap()
        dbgx = nc.dram_tensor("dbgx", [128, 8, 512], F32, kind="ExternalOutput").ap()
        dbgmv = nc.dram_tensor("dbgmv", [128, 2, 2], F32, kind="ExternalOutput").ap()
        dbgrs = nc.dram_tensor("dbgrs", [128, 2, 1], F32, kind="ExternalOutput").ap()
    x2d = nc.dram_tensor("x2d", [B, T, D], F32, kind="ExternalOutput").ap()
    lscr = nc.dram_tensor("lscr", [2, 1, 2, 512], F32, kind="Internal").ap()
    if dump_ffn:
        dbgwd = nc.dram_tensor("dbgwd", [128, 16, 8], F32, kind="ExternalOutput").ap()
        dbght = nc.dram_tensor("dbght", [128, 8, 2048], F32, kind="ExternalOutput").ap()
        dbgrc = nc.dram_tensor("dbgrc", [128, 16, 1024], F32, kind="ExternalOutput").ap()
        dbghs = nc.dram_tensor("dbghs", [128, 8, 512], F32, kind="ExternalOutput").ap()
        dbg_gate = {}
        for nm in ("glt", "gex", "gkeep", "gt8"):
            dbg_gate[nm] = nc.dram_tensor("dbg_" + nm, [128, 2, 8], F32, kind="ExternalOutput").ap()
        for nm in ("nm1", "nm2", "gssum", "grs"):
            dbg_gate[nm] = nc.dram_tensor("dbg_" + nm, [128, 2, 1], F32, kind="ExternalOutput").ap()
        dbg_gsc = nc.dram_tensor("dbg_gsc", [8, 512], F32, kind="ExternalOutput").ap()

    al = Alloc(nc)
    s = Sched()
    tk = {}

    def last(*names):
        return [tk[n] for n in names if n in tk]

    P_SIZE = 35 * KB - 512
    P0 = SB_TOP - P_SIZE

    def P(name, shape, dtype):
        return al.at(P0, name, shape, dtype, "P")

    ident = P("ident", [128, 128], BF16)
    identf8 = P("identf8", [8, 8], F32)
    onesAB = P("onesAB", [128, 2], BF16)
    onescol = P("onescol", [128, 1], BF16)
    statg = P("statg", [1, 64], F32)
    ones1f = P("ones1f", [128, 128], F32)
    gb_bc = P("gb_bc", [128, NE], F32)
    qqm = P("qqm", [1, 16], F32)
    kkm = P("kkm", [1, 16], F32)
    tmp11 = P("tmp11", [1, 2], F32)
    bnout = P("bnout", [128, 4, 12], F32)
    mv = P("mv", [128, 4, 2], F32)
    rstd = P("rstd", [128, 4, 1], F32)
    tln = P("tln", [128, 4, 1], F32)
    nbv = P("nbv", [128, 4, 1], F32)
    bost = P("bost", [1, 1024], F32)
    gbst = P("gbst", [1, 8], F32)
    XG = P("XG", [128, 2, 1024], F32)
    XN = P("XN", [128, 2, 1024], BF16)
    YST = P("YST", [128, 2, 1024], F32)
    LNO = P("LNO", [128, 2, 1024], F32)
    glt4 = P("glt4", [128, 4, NE], F32)
    glt4p = P("glt4p", [128, 4, 16], BF16)
    gm14 = P("gm14", [128, 4, 1], F32)
    gex4 = P("gex4", [128, 4, NE], F32)
    gk4 = P("gk4", [128, 4, NE], F32)
    gs4 = P("gs4", [128, 4, 1], F32)
    grs4 = P("grs4", [128, 4, 1], F32)
    epsc = P("epsc", [128, 1], F32)
    nc.const_aps.register(F32, EPS, epsc[:])

    def RS(name, shape, dtype):
        return al.at(BASE0, name, shape, dtype, "AS")

    QT = RS("QT", [128, 8, 2048], BF16)
    KT = RS("KT", [128, 8, 2048], BF16)
    VT = RS("VT", [128, 16, 16, 66], BF16)
    LBT = RS("LBT", [128, 128], BF16)
    RJ = RS("RJ", [128, 4, 512], BF16)
    bo_bc = RS("bo_bc", [128, 1024], F32)
    AS_END = al.cursors["AS"]

    def R1(name, shape, dtype):
        return al.at(AS_END, name, shape, dtype, "A1")

    WQS = R1("WQS", [128, 8, 1024], BF16)
    WKS = R1("WKS", [128, 8, 1024], BF16)
    WVS = R1("WVS", [128, 8, 1024], BF16)
    XN1T = R1("XN1T", [128, 8, 512], BF16)
    sqb = R1("sqb", [128, 8, 512], BF16)
    XA = R1("XA", [128, 2, 1024], BF16)

    def R2(name, shape, dtype):
        return al.at(AS_END, name, shape, dtype, "A2")

    WOS = R2("WOS", [128, 8, 1024], BF16)
    OGT = R2("OGT", [128, 8, 512], BF16)
    OGTtmp = R2("OGTtmp", [128, 2, 512], BF16)
    PTB = R2("PTB", [128, 6, 512], BF16)
    LNV = R2("LNV", [128, 4, 512], F32)
    SLB = R2("SLB", [64, 4, 512], F32)
    MC = R2("MC", [128, 8, 1], F32)
    OGU = R2("OGU", [128, 6, 512], F32)

    assert al.cursors["A1"] <= P0 and al.cursors["A2"] <= P0

    def RF(name, shape, dtype):
        return al.at(BASE0, name, shape, dtype, "F")

    RACC = RF("RACC", [128, 8, 1024], F32)
    HT = RF("HT", [128, 8, 1024], BF16)
    WD = RF("WD", [128, 16, NE], F32)
    F_END = al.cursors["F"]

    def RF1(name, shape, dtype):
        return al.at(F_END, name, shape, dtype, "F1")

    XN2T = RF1("XN2T", [128, 8, 1024], BF16)
    WSH = RF1("WSH", [128, 8, 2048], BF16)
    W1S = RF1("W1S", [128, 8, 2048], BF16)
    WSOS = RF1("WSOS", [128, 8, 1024], BF16)
    WGS = RF1("WGS", [128, 8, NE], BF16)
    HST = RF1("HST", [128, 8, 512], BF16)
    SILA = RF1("SILA", [128, 2, 512], BF16)
    gsc8 = RF1("gsc8", [16, 512], BF16)
    W2S = RF1("W2S", [128, 2, 8, 512], BF16)

    assert al.cursors["F1"] <= P0, al.cursors["F1"]

    pSA = nc.alloc_psum_tensor("pSA", [128, 1024], F32)
    pSB = nc.alloc_psum_tensor("pSB", [128, 1024], F32)
    pS = [pSA[:, 0:512], pSA[:, 512:1024], pSB[:, 0:512], pSB[:, 512:1024]]
    pSpair = [pSA, pSB]
    pOr = [nc.alloc_psum_tensor("pO0", [128, 512], F32),
           nc.alloc_psum_tensor("pO1", [128, 512], F32)]
    pB = nc.alloc_psum_tensor("pB", [128, 512], F32)
    pW = nc.alloc_psum_tensor("pW", [128, 512], F32)
    pOrb = [pOr[0].bitcast(BF16), pOr[1].bitcast(BF16)]

    prev = [None]

    def gchain(fn):
        prev[0] = s.emit("gp", fn, waits=[prev[0]], inc="gp")
        return prev[0]

    gchain(lambda g: g.memset(ident[:], 0.0))
    gchain(lambda g: g.affine_select(out=ident[:], in_=ident[:],
                                     compare_op=ALU.not_equal, fill=1.0, base=0,
                                     pattern=[[-1, 128]], channel_multiplier=1))
    gchain(lambda g: g.memset(identf8[:], 0.0))
    gchain(lambda g: g.affine_select(out=identf8[:], in_=identf8[:],
                                     compare_op=ALU.not_equal, fill=1.0, base=0,
                                     pattern=[[-1, 8]], channel_multiplier=1))
    gchain(lambda g: g.memset(onesAB[:], 0.0))
    gchain(lambda g: g.memset(onesAB[0:64, 0:1], 1.0))
    gchain(lambda g: g.memset(onescol[:], 1.0))
    gchain(lambda g: g.memset(onesAB[64:128, 1:2], 1.0))
    gchain(lambda g: g.memset(ones1f[:], 1.0))
    gchain(lambda g: g.memset(LBT[:], 1.0))
    gchain(lambda g: g.affine_select(
        out=LBT[:], in_=LBT[:], compare_op=ALU.is_ge,
        fill=0.0, base=0, pattern=[[1, 128]], channel_multiplier=-1))
    gchain(lambda g: g.memset(RJ[:], 0.0))
    for j in range(4):
        gchain(lambda g, j=j: g.affine_select(
            out=RJ[:, j, :], in_=RJ[:, j, :], compare_op=ALU.not_equal,
            fill=-2048.0, base=-128 * j + 1, pattern=[[1, 512]],
            channel_multiplier=-1))
        if j > 0:
            gchain(lambda g, j=j: g.memset(RJ[0:1, j, 0:128 * j], -2048.0))
    gchain(lambda g: g.memset(epsc[:], EPS))
    gchain(lambda g: g.memset(VT[:, :, :, 64:66], 0.0))
    tk["init"] = gchain(lambda g: g.memset(VT[:, :, :, 64:65], 1.0))
    INIT = tk["init"]

    tk["bo_ld"] = s.emit("sync", lambda e: e.dma_start(out=bost[:], in_=bo[:]),
                         inc="dxa", inc_by=16)
    tk["gb_ld"] = s.emit("sync", lambda e: e.dma_start(out=gbst[:], in_=gb[:]),
                         inc="dxb", inc_by=16)
    for half in range(2):
        tkp = s.emit("pe", lambda t, half=half: t.matmul(
            pB[:, 0:512], ones1f[0:1, :], bost[0:1, half * 512:(half + 1) * 512],
            start=True, stop=True),
            waits=[tk["bo_ld"], INIT] + last("bo_bc0"), inc="pe")
        tk["bo_bc%d" % half] = s.emit(
            "dve", lambda v, half=half: v.tensor_copy(
                bo_bc[:, half * 512:(half + 1) * 512], pB[:, 0:512]),
            waits=[tkp], inc="dve")
    tkp = s.emit("pe", lambda t: t.matmul(pB[:, 0:8], ones1f[0:1, :], gbst[:],
                                          start=True, stop=True),
                 waits=[tk["gb_ld"], tk["bo_bc0"], tk["bo_bc1"]], inc="pe")
    tk["gb_bc"] = s.emit("dve", lambda v: v.tensor_copy(gb_bc[:], pB[:, 0:8]),
                         waits=[tkp], inc="dve")
    pB_free = tk["gb_bc"]

    def ln_pipeline(par, src_ap, dst_ap, waits, out_name, f32out=False):
        sp = str(par)
        t1 = s.emit("dve", lambda v: v.reduce_sum(mv[:, par, 0:1], src_ap, axis=AX.X),
                    waits=list(waits) + last("ln_mv_rd" + sp), inc="dve")
        tq = s.emit("act", lambda a: a.activation(dst_ap, src_ap, AF.Square,
                                                  accum_out=mv[:, par, 1:2]),
                    waits=list(waits) + last("ln_mv_rdA" + sp), inc="act")
        t2a = s.emit("dve", lambda v: v.tensor_tensor(
            tln[:, par, :], mv[:, par, 0:1], mv[:, par, 0:1], op=ALU.mult),
            waits=[tq], inc="dve")
        t2 = s.emit("dve", lambda v: v.scalar_tensor_tensor(
            bnout[:, par, 0:1], tln[:, par, :], -1.0 / D, mv[:, par, 1:2],
            ALU.mult, ALU.add), inc="dve")
        tk["ln_mv_rdA" + sp] = t2
        t3 = s.emit("act", lambda a: a.activation(tln[:, par, :], bnout[:, par, 0:1],
                                                  AF.Ln, bias=EPS, scale=1.0 / D),
                    waits=[t2, INIT] + last("ln_tln_rd" + sp), inc="act")
        t4 = s.emit("act", lambda a: a.activation(rstd[:, par, :], tln[:, par, :],
                                                  AF.Exp, bias=0.0, scale=-0.5),
                    waits=last("ln_rstd_rd" + sp), inc="act")
        tk["ln_tln_rd" + sp] = t4
        t5 = s.emit("dve", lambda v: v.scalar_tensor_tensor(
            nbv[:, par, :], mv[:, par, 0:1], -1.0 / D, rstd[:, par, :],
            ALU.mult, ALU.mult),
            waits=[t4] + last("ln_nb_rd" + sp), inc="dve")
        tk["ln_mv_rd" + sp] = t5
        tk["ln_rstd_rd" + sp] = t5
        if f32out:
            t6 = s.emit("dve", lambda v: v.tensor_scalar(
                dst_ap, src_ap, rstd[:, par, :], nbv[:, par, :], ALU.mult, ALU.add),
                waits=[t5], inc="dve")
        else:
            t6 = s.emit("act", lambda a: a.activation(
                dst_ap, src_ap, AF.Identity,
                bias=nbv[:, par, :], scale=rstd[:, par, :]),
                waits=[t5], inc="act")
        tk["ln_nb_rd" + sp] = t6
        tk[out_name] = t6
        return t6

    tp_par = [0]

    def transpose_tile(src_tile, dst_ap, waits, split=False):
        par = tp_par[0] % 2
        tp_par[0] += 1
        plan = ((("sync", "tp"), ("act", "tr")) if split else (("sync", "tp"),))
        nh = len(plan)
        for h, (eng, pre) in enumerate(plan):
            sem = "%s%d" % (pre, par)
            t = s.emit(eng, lambda e, h=h, nh=nh: e.dma_start_transpose(
                out=dst_ap[:, h * (8 // nh):(h + 1) * (8 // nh), :],
                in_=src_tile[:, h * (1024 // nh):(h + 1) * (1024 // nh)]),
                waits=list(waits) + [s.tok(sem)], inc=sem, inc_by=16)
            tk["%s_last%d" % (pre, par)] = t
        return last("tp_last0", "tp_last1", "tr_last0", "tr_last1")

    def pe_transpose_tile(src_tile, dst_ap, waits, aa):
        pbk = pOrb[aa]
        tpe = None
        for cc in range(8):
            tpe = s.emit("pe", lambda t, cc=cc, pbk=pbk: t.transpose(
                pbk[:, cc * 128:(cc + 1) * 128],
                src_tile[:, cc * 128:(cc + 1) * 128], ident[:]),
                waits=(list(waits) + ([pO_free[aa]] if pO_free[aa] else []))
                if cc == 0 else [],
                inc="pe" if cc == 7 else None)
        tev = s.emit("dve", lambda v, pbk=pbk: v.tensor_copy(
            dst_ap, pbk[:].rearrange("p (c t) -> p c t", c=8)),
            waits=[tpe], inc="dve")
        pO_free[aa] = tev
        return tev

    LNOb = LNO.bitcast(BF16)
    XNS = None
    XG_free = [tk["bo_ld"], tk["gb_ld"]]
    XN_free = [[], [], [], []]
    pS_free = [None, None, None, None]
    pW_free = None
    pO_free = [None, None]

    for b in range(B):
        wq_tok = s.emit("gp", lambda g: g.dma_start(
            out=WQS[:], in_=wq.rearrange("(cc p) d -> p cc d", p=128)),
            waits=last("A2_done"), inc="swq", inc_by=16)
        wk_tok = s.emit("gp", lambda g: g.dma_start(
            out=WKS[:], in_=wk.rearrange("(cc p) d -> p cc d", p=128)),
            waits=last("A2_done"), inc="swk", inc_by=16)
        wv_tok = s.emit("gp", lambda g: g.dma_start(
            out=WVS[:], in_=wv.rearrange("(cc p) d -> p cc d", p=128)),
            waits=last("A2_done"), inc="swv", inc_by=16)

        xn1t_ready = []
        stat_jobs = []
        ip = 0
        XNS = [XN[:, 0, :], XN[:, 1, :], LNOb[:, 0, 0:1024], LNOb[:, 1, 0:1024]]
        for g in range(NG):
            hoist2 = (b == 1 and g == 0)
            for tt in (range(2, 4) if hoist2 else range(4)):
                tile = 4 * g + tt
                j = tile % 2
                lt = s.emit("sync", lambda e, b=b, tile=tile, j=j: e.dma_start(
                    out=XG[:, j, :], in_=x_in[b, tile * 128:(tile + 1) * 128, :]),
                    waits=[XG_free[j]], inc="dxa" if j == 0 else "dxb", inc_by=16)
                ln_pipeline(j, XG[:, j, :], XNS[tt],
                            [lt] + XN_free[tt] +
                            (last("lno_rd%d" % j) if tt >= 2 else []),
                            "xn%d" % tt)
                XG_free[j] = tk["xn%d" % tt]
            for tt in range(4):
                wtok = ([tk["xnh%d" % (tt % 2)]] if (hoist2 and tt < 2)
                        else [tk["xn%d" % tt]])
                tev = pe_transpose_tile(XNS[tt],
                                        XN1T[:, :, tt * 128:(tt + 1) * 128],
                                        wtok + last("qkv_done"), tt % 2)
                XN_free[tt] = [tev]
                if tt >= 2:
                    tk["lno_rd%d" % (tt % 2)] = tev
            xn1t_ready = [tev]

            for wi, (Wsrc, wtok, DT) in enumerate(
                    ((WQS, wq_tok, QT), (WKS, wk_tok, KT))):
                for oc in range(8):
                    i = ip % 4
                    ip += 1
                    tmm = None
                    for cc in range(8):
                        tmm = s.emit("pe", lambda t, Wsrc=Wsrc, oc=oc, cc=cc, i=i: t.matmul(
                            pS[i], Wsrc[:, cc, oc * 128:(oc + 1) * 128],
                            XN1T[:, cc, :], start=(cc == 0), stop=(cc == 7)),
                            waits=(xn1t_ready + [wtok] +
                                   ([pS_free[i]] if pS_free[i] else [])) if cc == 0 else [],
                            inc="pe" if cc == 7 else None)
                    tev = s.emit("dve", lambda v, DT=DT, oc=oc, g=g, i=i: v.tensor_copy(
                        DT[:, oc, g * 512:(g + 1) * 512], pS[i]),
                        waits=[tmm], inc="dve")
                    pS_free[i] = tev
                    tsq = s.emit("gp", lambda p, DT=DT, oc=oc, g=g: p.tensor_tensor(
                        sqb[:, oc, :], DT[:, oc, g * 512:(g + 1) * 512],
                        DT[:, oc, g * 512:(g + 1) * 512], op=ALU.mult),
                        waits=[tev] + last("sqb_rd%d" % oc), inc="gp")
                    stat_jobs.append((wi, oc, g, tsq))

            tev = None
            vidx = 0
            for tt in range(4):
                kc = 4 * g + tt
                for dh in range(2):
                    i = ip % 4
                    ip += 1
                    tmm = None
                    for cc in range(8):
                        tmm = s.emit("pe", lambda t, tt=tt, dh=dh, cc=cc, i=i: t.matmul(
                            pS[i], XN1T[:, cc, tt * 128:(tt + 1) * 128],
                            WVS[:, cc, dh * 512:(dh + 1) * 512],
                            start=(cc == 0), stop=(cc == 7)),
                            waits=(xn1t_ready + [wv_tok] +
                                   ([pS_free[i]] if pS_free[i] else [])) if cc == 0 else [],
                            inc="pe" if cc == 7 else None)
                    tev = s.emit("dve", lambda v, kc=kc, dh=dh, i=i: v.tensor_copy(
                        VT[:, kc, dh * 8:(dh + 1) * 8, 0:64],
                        pS[i].rearrange("p (h d) -> p h d", d=64)),
                        waits=[tmm, INIT], inc="dve")
                    pS_free[i] = tev
                    for sidx in (2 * vidx, 2 * vidx + 1):
                        if sidx >= len(stat_jobs):
                            continue
                        wi, oc, gs, tsq = stat_jobs[sidx]
                        pbank = pB if sidx % 2 == 0 else pW
                        tpp = s.emit("pe", lambda t, oc=oc, pbank=pbank: t.matmul(
                            pbank[0:1, :], onescol[:], sqb[:, oc, :],
                            start=True, stop=True),
                            waits=[tsq, INIT, pB_free if sidx % 2 == 0 else pW_free],
                            inc="pe")
                        tk["sqb_rd%d" % oc] = tpp
                        col = wi * 32 + gs * 8 + oc
                        trd = s.emit("dve", lambda v, col=col, pbank=pbank: v.tensor_reduce(
                            statg[0:1, col:col + 1], pbank[0:1, :], AX.X, ALU.max),
                            waits=[tpp], inc="dve")
                        if sidx % 2 == 0:
                            pB_free = trd
                        else:
                            pW_free = trd
                    vidx += 1
            tk["qkv_done"] = tev
            stat_jobs = []

        a1_done = tk["qkv_done"]
        sg = statg.rearrange("a (w g c) -> a w c g", w=2, c=8)
        for ci in range(8):
            s.emit("dve", lambda v, ci=ci: v.tensor_reduce(
                qqm[0:1, ci:ci + 1], sg[0:1, 0, ci, :], AX.X, ALU.max),
                waits=[a1_done], inc="dve")
            s.emit("dve", lambda v, ci=ci: v.tensor_reduce(
                kkm[0:1, ci:ci + 1], sg[0:1, 1, ci, :], AX.X, ALU.max), inc="dve")
        tqa = s.emit("dve", lambda v: v.tensor_tensor(
            kkm[0:1, 8:16], qqm[0:1, 0:8], kkm[0:1, 0:8], op=ALU.add),
            waits=last("tmp11b_rd"), inc="dve")
        tb = s.emit("pe", lambda t: t.matmul(
            pB[:, 0:8], ones1f[0:1, :], kkm[0:1, 8:16], start=True, stop=True),
            waits=[tqa, pB_free], inc="pe")
        tk["tmp11b_rd"] = tb
        tmc = s.emit("act", lambda a: a.activation(
            MC[:, 0:8, 0], pB[:, 0:8], AF.Identity, bias=0.0, scale=-BETA / 2),
            waits=[tb] + last(*["mc_rd%d" % c for c in range(8)]), inc="act")
        for ci in range(8):
            tk["mc_%d" % ci] = tmc
        pB_free = tmc
        if dump_qkv and b == 0:
            s.emit("gp", lambda g: g.dma_start(out=dbgq[:], in_=QT[:]),
                   waits=[a1_done], inc="dbg", inc_by=16)
            s.emit("gp", lambda g: g.dma_start(out=dbgk[:], in_=KT[:]),
                   waits=[a1_done], inc="dbg", inc_by=16)
            s.emit("gp", lambda g: g.dma_start(out=dbgv[:], in_=VT[:]),
                   waits=[a1_done], inc="dbg", inc_by=16)
            s.emit("gp", lambda g: g.dma_start(out=dbgx[:], in_=XN1T[:]),
                   waits=[a1_done], inc="dbg", inc_by=16)
            s.emit("gp", lambda g: g.dma_start(out=dbgmv[:], in_=mv[:]),
                   waits=[a1_done], inc="dbg", inc_by=16)
            s.emit("gp", lambda g: g.dma_start(out=dbgrs[:], in_=rstd[:]),
                   waits=[a1_done], inc="dbg", inc_by=16)

        if b == 0:
            for tt in range(2):
                j = tt % 2
                lt = s.emit("gp", lambda e, tile=tt, j=j: e.dma_start(
                    out=XA[:, j, :], in_=x_in[1, tile * 128:(tile + 1) * 128, :]),
                    waits=[], inc="hxa" if j == 0 else "hxb", inc_by=16)
                ln_pipeline(2 + j, XA[:, j, :], XN[:, j, :],
                            [lt] + XN_free[j], "xnh%d" % j)

        wo_tok = s.emit("gp", lambda g: g.dma_start(
            out=WOS[:], in_=wo.rearrange("(cc p) c -> p cc c", p=128)),
            waits=[a1_done], inc="swo", inc_by=16)

        DEPTH = 2
        jobs = []
        for g in range(NG):
            for ci in range(8):
                nkc = 4 * g + 4
                for kc in range(nkc):
                    jobs.append((g, ci, kc, nkc))
        PT_tok = [None] * len(jobs)
        PT_free = [None, None, None]
        ogu_free = [None] * 6
        ognorm_last = None
        nj = len(jobs)
        pr_idx = [0]

        def emit_S(j):
            nonlocal pB_free
            g, ci, kc, nkc = jobs[j]
            i2 = j % 2
            i4 = j % 3
            diag = kc >= 4 * g
            jm = kc - 4 * g
            tS = None
            for aa in range(2):
                hb = aa * 64
                tS = s.emit("pe", lambda t, hb=hb, ci=ci, g=g, kc=kc, i2=i2, aa=aa, diag=diag: t.matmul(
                    pS[2 * i2 + aa], KT[hb:hb + 64, ci, kc * 128:(kc + 1) * 128],
                    QT[hb:hb + 64, ci, g * 512:(g + 1) * 512],
                    start=True, stop=not diag, tile_position=(hb, 0)),
                    waits=([a1_done] + ([pS_free[i2]] if pS_free[i2] else []))
                    if aa == 0 else [],
                    inc=None if diag else ("pe" if aa == 1 else None))
            if diag:
                for aa in range(2):
                    tS = s.emit("pe", lambda t, i2=i2, aa=aa, jm=jm: t.matmul(
                        pS[2 * i2 + aa], LBT[:], RJ[:, jm, :],
                        start=False, stop=True),
                        waits=[INIT] if aa == 0 else [],
                        inc="pe" if aa == 1 else None)
            tE = s.emit("act", lambda a, i2=i2, i4=i4, ci=ci: a.activation(
                PTB[:, 2 * i4:2 * i4 + 2, :], pSpair[i2][:], AF.Exp,
                bias=MC[:, ci, :], scale=BETA),
                waits=[tS, tk["mc_%d" % ci]] + ([PT_free[i4]] if PT_free[i4] else []),
                inc="act")
            pS_free[i2] = tE
            tk["mc_rd%d" % ci] = tE
            PT_tok[j] = tE

        norm_q = []
        wo_q = []

        def advance():
            nonlocal pB_free, pW_free, ognorm_last
            if wo_q:
                e = wo_q[0]
                if e["st"] == 0:
                    emit_wo_mm(e)
                else:
                    emit_wo_epi(e)
                    wo_q.pop(0)
                    return
                e["st"] += 1
                return
            if norm_q:
                e = norm_q[0]
                st = e["st"]
                g, ci, rot = e["g"], e["ci"], e["rot"]
                sl0, sl1 = 2 * (rot % 3), 2 * (rot % 3) + 1
                l0, l1 = 2 * (rot % 2), 2 * (rot % 2) + 1
                if st == 0:
                    t_ = s.emit("act", lambda a, sl0=sl0, l0=l0: a.activation(
                        LNV[64:65, l0:l0 + 2, :], OGU[64:65, sl0:sl0 + 2, :],
                        AF.Ln),
                        waits=e["ev"] + last("lnv_rd%d" % l0, "lnv_rd%d" % l1),
                        inc="act")
                    e["ln"] = s.emit("act", lambda a, l0=l0: a.activation(
                        LNV[64:65, l0:l0 + 2, :], LNV[64:65, l0:l0 + 2, :],
                        AF.Exp, bias=0.0, scale=-1.0),
                        waits=[t_], inc="act")
                elif st == 1:
                    p = rot % 2
                    t_ = s.emit("sync", lambda ee, p=p, l0=l0: ee.dma_start(
                        out=lscr[p], in_=LNV[64:65, l0:l0 + 2, :]),
                        waits=[e["ln"]] + last("lscr_rd%d" % p),
                        inc="lh%d" % p, inc_by=16)
                    tk["lnv_rd%d" % l0] = t_
                    tk["lnv_rd%d" % l1] = t_
                    tk["lh_last%d" % p] = t_
                    e["h1"] = t_
                elif st == 2:
                    p = rot % 2
                    t_ = s.emit("sync", lambda ee, p=p, l0=l0: ee.dma_start(
                        out=SLB[0:64, l0:l0 + 2, :],
                        in_=lscr[p].broadcast_to((64, 2, 512))),
                        waits=[e["h1"]] + last("slb_rd%d" % l0, "slb_rd%d" % l1),
                        inc="lb%d" % p, inc_by=16)
                    tk["lscr_rd%d" % p] = t_
                    tk["lb_last%d" % p] = t_
                    e["sx"] = [t_, t_]
                elif st == 3:
                    tN = s.emit("gp", lambda v, ci=ci, sl0=sl0, l0=l0: v.tensor_tensor(
                        OGT[0:64, ci, :], OGU[0:64, sl0, :], SLB[0:64, l0, :],
                        op=ALU.mult), waits=[e["sx"][0]] + last("ogt_rd"), inc="gp")
                    tk["slb_rd%d" % l0] = tN
                    ogu_free[sl0] = tN
                    pp = rot % 2
                    tN2 = s.emit("gp", lambda v, sl1=sl1, l1=l1, pp=pp: v.tensor_tensor(
                        OGTtmp[0:64, pp, :], OGU[0:64, sl1, :], SLB[0:64, l1, :],
                        op=ALU.mult),
                        waits=[e["sx"][1]] + last("ogtmp_rd%d" % pp, "ogt_rd"), inc="gp")
                    tk["slb_rd%d" % l1] = tN2
                    tN2 = s.emit("gp", lambda ee, ci=ci, pp=pp: ee.dma_start(
                        out=OGT[64:128, ci, :], in_=OGTtmp[0:64, pp, :]),
                        waits=[tN2], inc="som%d" % pp, inc_by=16)
                    tk["ogtmp_rd%d" % pp] = tN2
                    tk["som_last%d" % pp] = tN2
                    ogu_free[sl1] = tN2
                    ognorm_last = tN
                    norm_q.pop(0)
                    if ci == 7:
                        for qt in range(4):
                            wo_q.append({"g": g, "qt": qt, "st": 0})
                    return
                e["st"] += 1
                return
        pr_idx = [0]

        def emit_V(j):
            nonlocal pB_free, ognorm_last
            g, ci, kc, nkc = jobs[j]
            i4 = j % 3
            advance()
            tV = None
            for aa in range(2):
                tV = s.emit("pe", lambda t, ci=ci, kc=kc, i4=i4, nkc=nkc, aa=aa: t.matmul(
                    pOr[aa][0:65, :], VT[:, kc, 2 * ci + aa, 0:65],
                    PTB[:, 2 * i4 + aa, :],
                    start=(kc == 0), stop=(kc == nkc - 1)),
                    waits=[PT_tok[j]] +
                    ([pO_free[aa]] if (kc == 0 and pO_free[aa]) else []),
                    inc="pe")
            PT_free[i4] = tV
            if kc != nkc - 1:
                return
            rot = pr_idx[0]
            pr_idx[0] += 1
            ev = []
            for aa in range(2):
                sl = 2 * (rot % 3) + aa
                tEv = s.emit("dve", lambda v, aa=aa, sl=sl: v.tensor_copy(
                    OGU[0:65, sl, :], pOr[aa][0:65, :]),
                    waits=[tV] + ([ogu_free[sl]] if ogu_free[sl] else []), inc="dve")
                pO_free[aa] = tEv
                ev.append(tEv)
            norm_q.append({"g": g, "ci": ci, "rot": rot, "st": 0, "ev": ev})

        def emit_wo_mm(e):
            nonlocal pW_free, pB_free
            g, qt = e["g"], e["qt"]
            tile = g * 4 + qt
            j = tile % 2
            e["lt"] = s.emit("sync", lambda ee, b=b, tile=tile, j=j: ee.dma_start(
                out=XG[:, j, :], in_=x_in[b, tile * 128:(tile + 1) * 128, :]),
                waits=[XG_free[j]], inc="dxa" if j == 0 else "dxb", inc_by=16)
            ty = []
            for half in range(2):
                tmm = None
                for ih in range(8):
                    tmm = s.emit("pe", lambda t, ih=ih, qt=qt, half=half: t.matmul(
                        pW[:],
                        OGT[:, ih, qt * 128:(qt + 1) * 128],
                        WOS[:, ih, half * 512:(half + 1) * 512],
                        start=(ih == 0), stop=(ih == 7)),
                        waits=([ognorm_last, wo_tok] +
                               last("som_last0", "som_last1") +
                               ([pW_free] if pW_free else []))
                        if ih == 0 else [],
                        inc="pe" if ih == 7 else None)
                if qt == 3 and half == 1:
                    tk["ogt_rd"] = tmm
                tyh = s.emit("dve", lambda v, j=j, half=half: v.scalar_tensor_tensor(
                    YST[:, j, half * 512:(half + 1) * 512], pW[:], 1.0,
                    bo_bc[:, half * 512:(half + 1) * 512], ALU.mult, ALU.add),
                    waits=[tmm] + (last("yst_rd%d" % j) if half == 0 else []) +
                    [tk["bo_bc0"], tk["bo_bc1"]], inc="dve")
                pW_free = tyh
                ty.append(tyh)
            e["ty"] = ty

        def emit_wo_epi(e):
            nonlocal pB_free
            g, qt = e["g"], e["qt"]
            tile = g * 4 + qt
            j = tile % 2
            ln_pipeline(j, YST[:, j, :], LNO[:, j, :],
                        [e["ty"][1]] + last("lno_rd%d" % j), "ln_y", f32out=True)
            tx2 = s.emit("dve", lambda v, j=j: v.tensor_add(
                YST[:, j, :], LNO[:, j, :], XG[:, j, :]),
                waits=[tk["ln_y"], e["lt"]], inc="dve")
            tk["lno_rd%d" % j] = tx2
            XG_free[j] = tx2
            tst = s.emit("sync", lambda ee, b=b, tile=tile, j=j: ee.dma_start(
                out=x2d[b, tile * 128:(tile + 1) * 128, :], in_=YST[:, j, :]),
                waits=[tx2], inc="dsa" if j == 0 else "dsb", inc_by=16)
            tk["yst_rd%d" % j] = tst

        for j in range(nj):
            emit_S(j)
            if j >= DEPTH:
                emit_V(j - DEPTH)
        for j in range(nj - DEPTH, nj):
            emit_V(j)
        while norm_q or wo_q:
            advance()

        if dump_qkv and b == 0:
            s.emit("gp", lambda g: g.dma_start(out=dbgo[:], in_=OGT[:]),
                   waits=[ognorm_last], inc="dbg", inc_by=16)
        tk["A2_done"] = s.tok("dve")

    att_done = [s.tok("pe"), s.tok("act"), s.tok("dve"), s.tok("gp"),
                s.tok("dsa"), s.tok("dsb")]
    stores_done = [s.tok("dsa"), s.tok("dsb")]
    w2buf_free = [None, None]
    wg_tok = s.emit("gp", lambda g: g.dma_start(
        out=WGS[:], in_=wg.rearrange("(cc p) e -> p cc e", p=128)),
        waits=att_done, inc="swg", inc_by=16)
    wso_tok = s.emit("gp", lambda g: g.dma_start(
        out=WSOS[:], in_=wso.rearrange("(hc p) c -> p hc c", p=128)),
        waits=att_done, inc="swso", inc_by=16)
    wsi_tok = [s.emit("gp", lambda q: q.dma_start(
        out=WSH[:], in_=wsi.rearrange("(cc p) f -> p cc f", p=128)),
        waits=att_done, inc="swsh", inc_by=16)]
    w1_tok = [s.emit("gp", lambda q: q.dma_start(
        out=W1S[:], in_=w1.rearrange("(cc p) f -> p cc f", p=128)),
        waits=att_done, inc="swsh", inc_by=16)]
    tk["gsc8z"] = s.emit("gp", lambda g: g.memset(gsc8[:], 0.0),
                         waits=att_done, inc="gp")
    tk["gsc8_rd"] = tk["gsc8z"]

    def emit_chains01(b, hf):
        for tt in range(2):
            tile = 8 * hf + tt
            j = tile % 2
            lt = s.emit("sync", lambda e, b=b, tile=tile, j=j: e.dma_start(
                out=XG[:, j, :], in_=x2d[b, tile * 128:(tile + 1) * 128, :]),
                waits=[XG_free[j]] + (stores_done if tile == 0 else []),
                inc="dxa" if j == 0 else "dxb", inc_by=16)
            ln_pipeline(2 + j, XG[:, j, :], XN[:, j, :],
                        [lt] + XN_free[tt], "xn2_%d" % tt)
            XG_free[j] = tk["xn2_%d" % tt]

    def emit_tiles(b, hf, skip01=False):
        bar = att_done if b == 0 else []
        XNSF = [XN[:, 0, :], XN[:, 1, :],
                HST[:, 0:2, :].rearrange("p a b -> p (a b)"),
                HST[:, 2:4, :].rearrange("p a b -> p (a b)")]
        xn2rg = []
        for g0 in range(2):
            g = 2 * hf + g0
            for tt in range(4):
                if skip01 and g0 == 0 and tt < 2:
                    continue
                tile = 4 * g + tt
                j = tile % 2
                lt = s.emit("sync", lambda e, b=b, tile=tile, j=j: e.dma_start(
                    out=XG[:, j, :], in_=x2d[b, tile * 128:(tile + 1) * 128, :]),
                    waits=[XG_free[j]] + (stores_done if tile == 0 else []),
                    inc="dxa" if j == 0 else "dxb", inc_by=16)
                ln_pipeline(2 + j, XG[:, j, :], XNSF[tt],
                            [lt] + XN_free[tt] +
                            (last("hst_rd") + (bar if tile < 8 else [])
                             if tt >= 2 else []), "xn2_%d" % tt)
                XG_free[j] = tk["xn2_%d" % tt]
            te = None
            for tt in range(4):
                tile = 4 * g + tt
                tl = tile - 8 * hf
                w_ = ([tk["xn2_%d" % tt]] + (bar if tile == 0 else []) +
                      (last("xn2t_rd") if tl == 0 else []))
                tev = pe_transpose_tile(
                    XNSF[tt], XN2T[:, :, tl * 128:(tl + 1) * 128],
                    w_, tt % 2)
                te = [tev]
                XN_free[tt] = te
                if tt >= 2:
                    tk["hst_rd"] = tev
            xn2rg.append(te)
        return xn2rg

    halves = [(bb, hh) for bb in range(B) for hh in range(2)]
    xn2pend = {halves[0]: emit_tiles(*halves[0])}
    ht_ready = None
    for idx, (b, hf) in enumerate(halves):
        bar = att_done if b == 0 else []
        xn2rg = xn2pend.pop((b, hf))
        if idx + 1 < len(halves):
            emit_chains01(*halves[idx + 1])

        def swiglu_group(wtok, wsrc, g0, g, dst_fn, war_name):
            th = None
            for fc in range(8):
                ia = sw_ip[0] % 4
                ib_ = (sw_ip[0] + 1) % 4
                sw_ip[0] += 2
                ta = None
                for cc in range(8):
                    ta = s.emit("pe", lambda t, fc=fc, cc=cc, g0=g0, ia=ia, wsrc=wsrc: t.matmul(
                        pS[ia], wsrc[:, cc, fc * 128:(fc + 1) * 128],
                        XN2T[:, cc, g0 * 512:(g0 + 1) * 512],
                        start=(cc == 0), stop=(cc == 7)),
                        waits=(wtok + xn2_ready + ([pS_free[ia]] if pS_free[ia] else []))
                        if cc == 0 else [],
                        inc="pe" if cc == 7 else None)
                tb = None
                for cc in range(8):
                    tb = s.emit("pe", lambda t, fc=fc, cc=cc, g0=g0, ib_=ib_, wsrc=wsrc: t.matmul(
                        pS[ib_], wsrc[:, cc, 1024 + fc * 128:1024 + (fc + 1) * 128],
                        XN2T[:, cc, g0 * 512:(g0 + 1) * 512],
                        start=(cc == 0), stop=(cc == 7)),
                        waits=([pS_free[ib_]] if pS_free[ib_] else [])
                        if cc == 0 else [],
                        inc="pe" if cc == 7 else None)
                si = fc % 2
                tsil = s.emit("act", lambda a, ia=ia, si=si: a.activation(
                    SILA[:, si, :], pS[ia], AF.Silu),
                    waits=[ta] + ([sila_free[si]] if sila_free[si] else []), inc="act")
                pS_free[ia] = tsil
                th = s.emit("dve", lambda v, fc=fc, g=g, si=si, ib_=ib_: v.tensor_tensor(
                    dst_fn(fc, g), SILA[:, si, :], pS[ib_], op=ALU.mult),
                    waits=[tb, tsil] + (last(war_name) if fc == 0 else []), inc="dve")
                sila_free[si] = th
                pS_free[ib_] = th
            return th

        sila_free = [None, None]
        sw_ip = [0]
        for g0 in range(2):
            g = 2 * hf + g0
            xn2_ready = xn2rg[g0]

            tgm = None
            for cc in range(8):
                tgm = s.emit("pe", lambda t, cc=cc, g0=g0: t.matmul(
                    pB[0:8, :], WGS[:, cc, :], XN2T[:, cc, g0 * 512:(g0 + 1) * 512],
                    start=(cc == 0), stop=(cc == 7)),
                    waits=(xn2_ready + [wg_tok, pB_free]) if cc == 0 else [],
                    inc="pe" if cc == 7 else None)
            tge = s.emit("dve", lambda v: v.tensor_copy(gsc8[0:8, :], pB[0:8, :]),
                         waits=[tgm] + last("gsc8_rd"), inc="dve")
            pB_free = tge
            pgx = g % 2
            tgx = s.emit("sync", lambda e: e.dma_start_transpose(
                out=glt4p[:], in_=gsc8[:]),
                waits=[tge] + last("glt4p_rd", "gsc8z") + [s.tok("gx%d" % pgx)],
                inc="gx%d" % pgx, inc_by=16)
            tk["gsc8_rd"] = tgx
            tk["gx_last%d" % pgx] = tgx
            tgl = s.emit("dve", lambda v: v.tensor_tensor(
                glt4[:], glt4p[:, :, 0:NE],
                gb_bc[:].rearrange("p (x e) -> p x e", x=1).broadcast_to([128, 4, NE]),
                op=ALU.add),
                waits=[tgx, tk["gb_bc"]] + last("glt_rd"), inc="dve")
            tk["glt4p_rd"] = tgl
            t1 = s.emit("dve", lambda v: v.reduce_max(
                gm14[:], glt4[:], axis=AX.X), waits=[tgl], inc="dve")
            tx = s.emit("act", lambda a: a.activation(gex4[:], glt4[:], AF.Exp),
                        waits=[tgl] + last("gex_rd"), inc="act")
            ts_ = s.emit("dve", lambda v: v.reduce_sum(
                gs4[:], gex4[:], axis=AX.X), waits=[tx], inc="dve")
            t3 = s.emit("dve", lambda v: v.tensor_tensor(
                gk4[:], glt4[:], gm14[:].broadcast_to([128, 4, 8]),
                op=ALU.is_ge), waits=[t1], inc="dve")
            t4 = s.emit("dve", lambda v: v.scalar_tensor_tensor(
                gk4[:], gk4[:], -1e30, glt4[:], ALU.mult, ALU.add), inc="dve")
            t5 = s.emit("dve", lambda v: v.reduce_max(
                gm14[:], gk4[:], axis=AX.X), inc="dve")
            t6 = s.emit("dve", lambda v: v.tensor_tensor(
                gk4[:], glt4[:], gm14[:].broadcast_to([128, 4, 8]), op=ALU.is_ge), inc="dve")
            tk["glt_rd"] = t6
            t7 = s.emit("dve", lambda v: v.tensor_tensor(
                gk4[:], gk4[:], gex4[:], op=ALU.mult), inc="dve")
            tk["gex_rd"] = t7
            t8 = s.emit("dve", lambda v: v.reciprocal(grs4[:], gs4[:]),
                        waits=[ts_], inc="dve")
            t9 = s.emit("dve", lambda v, g=g: v.tensor_tensor(
                WD[:, g * 4:(g + 1) * 4, :], gk4[:], grs4[:].broadcast_to([128, 4, 8]),
                op=ALU.mult), inc="dve")
            for qt in range(4):
                tk["wd_%d" % (g * 4 + qt)] = t9
            tk["gsc8_rd"] = t9

            th = swiglu_group(wsi_tok, WSH, g0, g,
                              lambda fc, g: HST[:, fc, :], "hst_rd")
            for qt in range(4):
                tile = g * 4 + qt
                tloc = tile - 8 * hf
                trc = None
                for half in range(2):
                    tmm = None
                    for hc in range(8):
                        tmm = s.emit("pe", lambda t, hc=hc, qt=qt, half=half: t.matmul(
                            pW[:],
                            HST[:, hc, qt * 128:(qt + 1) * 128],
                            WSOS[:, hc, half * 512:(half + 1) * 512],
                            start=(hc == 0), stop=(hc == 7)),
                            waits=([th, wso_tok] +
                                   ([pW_free] if pW_free else []))
                            if hc == 0 else [],
                            inc="pe" if hc == 7 else None)
                    trc = s.emit("dve", lambda v, tloc=tloc, half=half: v.tensor_copy(
                        RACC[:, tloc, half * 512:(half + 1) * 512], pW[:]),
                        waits=[tmm] + (bar if (tile == 0 and half == 0) else []),
                        inc="dve")
                    pW_free = trc
                tk["racc_%d" % tile] = trc
            tk["hst_rd"] = tmm

        for g0 in range(2):
            g = 2 * hf + g0
            xn2_ready = xn2rg[g0]
            ht_ready = swiglu_group(
                w1_tok, W1S, g0, g,
                lambda fc, g: HT[:, fc, (g % 2) * 512:((g % 2) + 1) * 512],
                "ht_rd")
        tk["xn2t_rd"] = s.tok("pe")

        if idx + 1 < len(halves):
            xn2pend[halves[idx + 1]] = emit_tiles(*halves[idx + 1],
                                                  skip01=True)

        for e in range(NE):
            for ch in range(2):
                ib = (e * 2 + ch) % 2
                wt = s.emit("gp", lambda g, e=e, ch=ch, ib=ib: g.dma_start(
                    out=W2S[:, ib, :, :],
                    in_=w2[e].rearrange("(hc p) c -> p hc c", p=128)[:, :, ch * 512:(ch + 1) * 512]),
                    waits=[w2buf_free[ib]] if w2buf_free[ib] else att_done,
                    inc="sw2a" if ib == 0 else "sw2b", inc_by=16)
                for qt in range(8):
                    tile = hf * 8 + qt
                    i2 = qt % 4
                    tmm = None
                    for hc in range(8):
                        tmm = s.emit("pe", lambda t, hc=hc, qt=qt, ib=ib, i2=i2: t.matmul(
                            pS[i2], HT[:, hc, qt * 128:(qt + 1) * 128],
                            W2S[:, ib, hc, :], start=(hc == 0), stop=(hc == 7)),
                            waits=([wt, ht_ready] +
                                   ([pS_free[i2]] if pS_free[i2] else []))
                            if hc == 0 else [],
                            inc="pe" if hc == 7 else None)
                    tup = s.emit("dve",
                                 lambda v, e=e, qt=qt, tile=tile, ch=ch, i2=i2: v.scalar_tensor_tensor(
                                     RACC[:, qt, ch * 512:(ch + 1) * 512], pS[i2],
                                     WD[:, tile, e:e + 1],
                                     RACC[:, qt, ch * 512:(ch + 1) * 512],
                                     ALU.mult, ALU.add),
                                 waits=[tmm, tk["wd_%d" % tile], tk["racc_%d" % tile]],
                                 inc="dve")
                    tk["racc_%d" % tile] = tup
                    pS_free[i2] = tup
                    if e == NE - 1 and ch == 1:
                        j = tile % 2
                        lt = s.emit("gp", lambda ee, b=b, tile=tile, j=j: ee.dma_start(
                            out=YST[:, j, :], in_=x2d[b, tile * 128:(tile + 1) * 128, :]),
                            waits=last("yst_rd%d" % j), inc="fxa" if j == 0 else "fxb",
                            inc_by=16)
                        ln_pipeline(j, RACC[:, qt, :], LNO[:, j, :],
                                    [tup] + last("lno_rd%d" % j), "ln_f", f32out=True)
                        tof = s.emit("dve", lambda v, j=j: v.tensor_add(
                            YST[:, j, :], LNO[:, j, :], YST[:, j, :]),
                            waits=[tk["ln_f"], lt], inc="dve")
                        tk["lno_rd%d" % j] = tof
                        tst = s.emit("gp", lambda ee, b=b, tile=tile, j=j: ee.dma_start(
                            out=out[b, tile * 128:(tile + 1) * 128, :], in_=YST[:, j, :]),
                            waits=[tof], inc="fsa" if j == 0 else "fsb", inc_by=16)
                        tk["yst_rd%d" % j] = tst
                w2buf_free[ib] = tmm
        tk["ht_rd"] = tmm

    tk["f2_pe"] = s.tok("pe")
    tk["F3_done"] = s.tok("fsa")
    tk["F3_doneb"] = s.tok("fsb")

    s.emit("sync", lambda e: e.nop(),
           waits=[s.tok("dsa"), s.tok("dsb"), s.tok("dbg"),
                  s.tok("tp0"), s.tok("tp1"), s.tok("som0"), s.tok("som1"),
                  s.tok("lh0"), s.tok("lh1"), s.tok("lb0"), s.tok("lb1"),
                  s.tok("fxa"), s.tok("fxb"), s.tok("fsa"), s.tok("fsb"),
                  s.tok("gx0"), s.tok("gx1"), s.tok("hxa"), s.tok("hxb"),
                  s.tok("tr0"), s.tok("tr1"),
                  s.tok("swsh2"), s.tok("swsh3")])

    from contextlib import ExitStack
    with ExitStack() as ctx:
        sems = {}
        for name in ("pe", "act", "dve", "gp", "swq", "swk", "swv", "swo",
                     "swg", "swsh", "swsh2", "swsh3", "swso", "sw2a", "sw2b",
                     "dxa", "dxb", "dsa", "dsb", "dbg",
                     "tp0", "tp1", "som0", "som1",
                     "lh0", "lh1", "lb0", "lb1",
                     "fxa", "fxb", "fsa", "fsb", "gx0", "gx1",
                     "hxa", "hxb", "tr0", "tr1"):
            sems[name] = ctx.enter_context(nc.semaphore(name))
        blk = ctx.enter_context(nc.Block())
        s.replay(blk, sems)
    return nc


def kernel(L0_attn_x, L0_attn_Wq, L0_attn_Wk, L0_attn_Wv, L0_attn_Wo, L0_attn_bo,
           L0_ffn_w_shared_in, L0_ffn_w_shared_out, L0_ffn_w1_shared, L0_ffn_w2,
           L0_ffn_w_gate, L0_ffn_gate_bias):
    x = np.ascontiguousarray(np.asarray(L0_attn_x, dtype=np.float32))
    weights = dict(
        wq=np.ascontiguousarray(np.asarray(L0_attn_Wq, np.float32)),
        wk=np.ascontiguousarray(np.asarray(L0_attn_Wk, np.float32)),
        wv=np.ascontiguousarray(np.asarray(L0_attn_Wv, np.float32)),
        wo=np.ascontiguousarray(np.asarray(L0_attn_Wo, np.float32)),
        bo=np.ascontiguousarray(np.asarray(L0_attn_bo, np.float32).reshape(1, D)),
        wsi=np.ascontiguousarray(np.asarray(L0_ffn_w_shared_in, np.float32)),
        wso=np.ascontiguousarray(np.asarray(L0_ffn_w_shared_out, np.float32)),
        w1=np.ascontiguousarray(np.asarray(L0_ffn_w1_shared, np.float32)),
        w2=np.ascontiguousarray(np.asarray(L0_ffn_w2, np.float32)),
        wg=np.ascontiguousarray(np.asarray(L0_ffn_w_gate, np.float32)),
        gb=np.ascontiguousarray(np.asarray(L0_ffn_gate_bias, np.float32).reshape(1, NE)),
    )
    nc = build_program()
    in_maps = []
    for i in range(NCORES):
        m = dict(weights)
        m["x_in"] = np.ascontiguousarray(x[2 * i:2 * i + 2])
        in_maps.append(m)
    res = run_bass_kernel_spmd(nc, in_maps, core_ids=list(range(NCORES)))
    return np.concatenate([r["out"] for r in res.results], axis=0)


# version 39
# speedup vs baseline: 1.0170x; 1.0110x over previous
import numpy as np
import concourse.bass as bass
import concourse.mybir as mybir
from concourse.bass_utils import run_bass_kernel_spmd

F32 = mybir.dt.float32
BF16 = mybir.dt.bfloat16
AF = mybir.ActivationFunctionType
ALU = mybir.AluOpType
AX = mybir.AxisListType

NCORES = 8
B = 2
T = 2048
D = 1024
H = 16
NT = 16
NG = 4
NE = 8
FF = 2048
BETA = 0.125
EPS = 1e-5
KB = 1024
SB_TOP = 229344
BASE0 = 16512


class Sched:
    ENGINES = ("pe", "act", "dve", "gp", "sync")

    def __init__(self):
        self.items = {e: [] for e in self.ENGINES}
        self.counts = {}

    def tok(self, sem):
        return (sem, self.counts.get(sem, 0))

    SERIALIZE = ("dve",)

    def emit(self, eng, fn, waits=(), inc=None, inc_by=1):
        w = [(sm, v) for t in waits if t is not None for (sm, v) in [t] if v > 0]
        if eng in self.SERIALIZE:
            w.append((eng, self.counts.get(eng, 0)))
            if inc is None:
                inc = eng
        tok = None
        if inc is not None:
            self.counts[inc] = self.counts.get(inc, 0) + inc_by
            tok = (inc, self.counts[inc])
        self.items[eng].append((fn, tuple(w), inc, inc_by))
        return tok

    def replay(self, blk, sems):
        def mkrun(name):
            def run(eng_obj):
                last = {}
                for fn, waits, inc, inc_by in self.items[name]:
                    for sm, v in waits:
                        if last.get(sm, 0) < v:
                            eng_obj.wait_ge(sems[sm], v)
                            last[sm] = v
                    ins = fn(eng_obj)
                    if inc is not None:
                        ins.then_inc(sems[inc], inc_by)
            return run
        if self.items["gp"]:
            blk.gpsimd(mkrun("gp"))
        if self.items["pe"]:
            blk.tensor(mkrun("pe"))
        if self.items["act"]:
            blk.scalar(mkrun("act"))
        if self.items["dve"]:
            blk.vector(mkrun("dve"))
        if self.items["sync"]:
            blk.sync(mkrun("sync"))


class Alloc:
    def __init__(self, nc):
        self.nc = nc
        self.cursors = {}

    def at(self, region_base, name, shape, dtype, cursor_key):
        cur = self.cursors.get(cursor_key, region_base)
        cur = (cur + 31) & ~31
        th = self.nc.alloc_sbuf_tensor_at(name, list(shape), dtype, offset=cur)
        per_part = int(np.prod(shape[1:])) * (2 if dtype == BF16 else 4)
        self.cursors[cursor_key] = cur + per_part
        assert self.cursors[cursor_key] <= SB_TOP, (
            f"SBUF overflow: {name} ends at {self.cursors[cursor_key]}")
        return th


def build_program(detect_races=False, dump_qkv=False, dump_ffn=False):
    nc = bass.Bass("TRN2", target_bir_lowering=False,
                   detect_race_conditions=detect_races)

    x_in = nc.dram_tensor("x_in", [B, T, D], F32, kind="ExternalInput").ap()
    wq = nc.dram_tensor("wq", [D, D], F32, kind="ExternalInput").ap()
    wk = nc.dram_tensor("wk", [D, D], F32, kind="ExternalInput").ap()
    wv = nc.dram_tensor("wv", [D, D], F32, kind="ExternalInput").ap()
    wo = nc.dram_tensor("wo", [D, D], F32, kind="ExternalInput").ap()
    bo = nc.dram_tensor("bo", [1, D], F32, kind="ExternalInput").ap()
    wsi = nc.dram_tensor("wsi", [D, FF], F32, kind="ExternalInput").ap()
    wso = nc.dram_tensor("wso", [D, D], F32, kind="ExternalInput").ap()
    w1 = nc.dram_tensor("w1", [D, FF], F32, kind="ExternalInput").ap()
    w2 = nc.dram_tensor("w2", [NE, D, D], F32, kind="ExternalInput").ap()
    wg = nc.dram_tensor("wg", [D, NE], F32, kind="ExternalInput").ap()
    gb = nc.dram_tensor("gb", [1, NE], F32, kind="ExternalInput").ap()
    out = nc.dram_tensor("out", [B, T, D], F32, kind="ExternalOutput").ap()
    if dump_qkv:
        dbgq = nc.dram_tensor("dbgq", [128, 8, 2048], F32, kind="ExternalOutput").ap()
        dbgk = nc.dram_tensor("dbgk", [128, 8, 2048], F32, kind="ExternalOutput").ap()
        dbgv = nc.dram_tensor("dbgv", [128, 16, 16, 66], F32, kind="ExternalOutput").ap()
        dbgo = nc.dram_tensor("dbgo", [64, 16, 512], F32, kind="ExternalOutput").ap()
        dbgx = nc.dram_tensor("dbgx", [128, 8, 512], F32, kind="ExternalOutput").ap()
        dbgmv = nc.dram_tensor("dbgmv", [128, 2, 2], F32, kind="ExternalOutput").ap()
        dbgrs = nc.dram_tensor("dbgrs", [128, 2, 1], F32, kind="ExternalOutput").ap()
    x2d = nc.dram_tensor("x2d", [B, T, D], F32, kind="ExternalOutput").ap()
    lscr = nc.dram_tensor("lscr", [2, 1, 2, 512], F32, kind="Internal").ap()
    if dump_ffn:
        dbgwd = nc.dram_tensor("dbgwd", [128, 16, 8], F32, kind="ExternalOutput").ap()
        dbght = nc.dram_tensor("dbght", [128, 8, 2048], F32, kind="ExternalOutput").ap()
        dbgrc = nc.dram_tensor("dbgrc", [128, 16, 1024], F32, kind="ExternalOutput").ap()
        dbghs = nc.dram_tensor("dbghs", [128, 8, 512], F32, kind="ExternalOutput").ap()
        dbg_gate = {}
        for nm in ("glt", "gex", "gkeep", "gt8"):
            dbg_gate[nm] = nc.dram_tensor("dbg_" + nm, [128, 2, 8], F32, kind="ExternalOutput").ap()
        for nm in ("nm1", "nm2", "gssum", "grs"):
            dbg_gate[nm] = nc.dram_tensor("dbg_" + nm, [128, 2, 1], F32, kind="ExternalOutput").ap()
        dbg_gsc = nc.dram_tensor("dbg_gsc", [8, 512], F32, kind="ExternalOutput").ap()

    al = Alloc(nc)
    s = Sched()
    tk = {}

    def last(*names):
        return [tk[n] for n in names if n in tk]

    P_SIZE = 35 * KB - 512
    P0 = SB_TOP - P_SIZE

    def P(name, shape, dtype):
        return al.at(P0, name, shape, dtype, "P")

    ident = P("ident", [128, 128], BF16)
    identf8 = P("identf8", [8, 8], F32)
    onesAB = P("onesAB", [128, 2], BF16)
    onescol = P("onescol", [128, 1], BF16)
    statg = P("statg", [1, 64], F32)
    ones1f = P("ones1f", [128, 128], F32)
    gb_bc = P("gb_bc", [128, NE], F32)
    qqm = P("qqm", [1, 16], F32)
    kkm = P("kkm", [1, 16], F32)
    tmp11 = P("tmp11", [1, 2], F32)
    bnout = P("bnout", [128, 4, 12], F32)
    mv = P("mv", [128, 4, 2], F32)
    rstd = P("rstd", [128, 4, 1], F32)
    tln = P("tln", [128, 4, 1], F32)
    nbv = P("nbv", [128, 4, 1], F32)
    bost = P("bost", [1, 1024], F32)
    gbst = P("gbst", [1, 8], F32)
    XG = P("XG", [128, 2, 1024], F32)
    XN = P("XN", [128, 2, 1024], BF16)
    YST = P("YST", [128, 2, 1024], F32)
    LNO = P("LNO", [128, 2, 1024], F32)
    glt4 = P("glt4", [128, 4, NE], F32)
    glt4p = P("glt4p", [128, 4, 16], BF16)
    gm14 = P("gm14", [128, 4, 1], F32)
    gex4 = P("gex4", [128, 4, NE], F32)
    gk4 = P("gk4", [128, 4, NE], F32)
    gs4 = P("gs4", [128, 4, 1], F32)
    grs4 = P("grs4", [128, 4, 1], F32)
    epsc = P("epsc", [128, 1], F32)
    nc.const_aps.register(F32, EPS, epsc[:])

    def RS(name, shape, dtype):
        return al.at(BASE0, name, shape, dtype, "AS")

    QT = RS("QT", [128, 8, 2048], BF16)
    KT = RS("KT", [128, 8, 2048], BF16)
    VT = RS("VT", [128, 16, 16, 66], BF16)
    LBT = RS("LBT", [128, 128], BF16)
    RJ = RS("RJ", [128, 4, 512], BF16)
    bo_bc = RS("bo_bc", [128, 1024], F32)
    AS_END = al.cursors["AS"]

    def R1(name, shape, dtype):
        return al.at(AS_END, name, shape, dtype, "A1")

    WQS = R1("WQS", [128, 8, 1024], BF16)
    WKS = R1("WKS", [128, 8, 1024], BF16)
    WVS = R1("WVS", [128, 8, 1024], BF16)
    XN1T = R1("XN1T", [128, 8, 512], BF16)
    sqb = R1("sqb", [128, 8, 512], BF16)
    XA = R1("XA", [128, 2, 1024], BF16)

    def R2(name, shape, dtype):
        return al.at(AS_END, name, shape, dtype, "A2")

    WOS = R2("WOS", [128, 8, 1024], BF16)
    OGT = R2("OGT", [128, 8, 512], BF16)
    OGTtmp = R2("OGTtmp", [128, 2, 512], BF16)
    PTB = R2("PTB", [128, 6, 512], BF16)
    LNV = R2("LNV", [128, 4, 512], F32)
    SLB = R2("SLB", [64, 4, 512], F32)
    MC = R2("MC", [128, 8, 1], F32)
    OGU = R2("OGU", [128, 6, 512], F32)

    assert al.cursors["A1"] <= P0 and al.cursors["A2"] <= P0

    def RF(name, shape, dtype):
        return al.at(BASE0, name, shape, dtype, "F")

    RACC = RF("RACC", [128, 8, 1024], F32)
    HT = RF("HT", [128, 8, 1024], BF16)
    WD = RF("WD", [128, 16, NE], F32)
    F_END = al.cursors["F"]

    def RF1(name, shape, dtype):
        return al.at(F_END, name, shape, dtype, "F1")

    XN2T = RF1("XN2T", [128, 8, 1024], BF16)
    WSH = RF1("WSH", [128, 8, 2048], BF16)
    W1S = RF1("W1S", [128, 8, 2048], BF16)
    WSOS = RF1("WSOS", [128, 8, 1024], BF16)
    WGS = RF1("WGS", [128, 8, NE], BF16)
    HST = RF1("HST", [128, 8, 512], BF16)
    SILA = RF1("SILA", [128, 2, 512], BF16)
    gsc8 = RF1("gsc8", [16, 512], BF16)
    W2S = RF1("W2S", [128, 2, 8, 512], BF16)

    assert al.cursors["F1"] <= P0, al.cursors["F1"]

    pSA = nc.alloc_psum_tensor("pSA", [128, 1024], F32)
    pSB = nc.alloc_psum_tensor("pSB", [128, 1024], F32)
    pS = [pSA[:, 0:512], pSA[:, 512:1024], pSB[:, 0:512], pSB[:, 512:1024]]
    pSpair = [pSA, pSB]
    pOr = [nc.alloc_psum_tensor("pO0", [128, 512], F32),
           nc.alloc_psum_tensor("pO1", [128, 512], F32)]
    pB = nc.alloc_psum_tensor("pB", [128, 512], F32)
    pW = nc.alloc_psum_tensor("pW", [128, 512], F32)
    pOrb = [pOr[0].bitcast(BF16), pOr[1].bitcast(BF16)]

    prev = [None]

    def gchain(fn):
        prev[0] = s.emit("gp", fn, waits=[prev[0]], inc="gp")
        return prev[0]

    gchain(lambda g: g.memset(ident[:], 0.0))
    gchain(lambda g: g.affine_select(out=ident[:], in_=ident[:],
                                     compare_op=ALU.not_equal, fill=1.0, base=0,
                                     pattern=[[-1, 128]], channel_multiplier=1))
    gchain(lambda g: g.memset(identf8[:], 0.0))
    gchain(lambda g: g.affine_select(out=identf8[:], in_=identf8[:],
                                     compare_op=ALU.not_equal, fill=1.0, base=0,
                                     pattern=[[-1, 8]], channel_multiplier=1))
    gchain(lambda g: g.memset(onesAB[:], 0.0))
    gchain(lambda g: g.memset(onesAB[0:64, 0:1], 1.0))
    gchain(lambda g: g.memset(onescol[:], 1.0))
    gchain(lambda g: g.memset(onesAB[64:128, 1:2], 1.0))
    gchain(lambda g: g.memset(ones1f[:], 1.0))
    gchain(lambda g: g.memset(LBT[:], 1.0))
    gchain(lambda g: g.affine_select(
        out=LBT[:], in_=LBT[:], compare_op=ALU.is_ge,
        fill=0.0, base=0, pattern=[[1, 128]], channel_multiplier=-1))
    gchain(lambda g: g.memset(RJ[:], 0.0))
    for j in range(4):
        gchain(lambda g, j=j: g.affine_select(
            out=RJ[:, j, :], in_=RJ[:, j, :], compare_op=ALU.not_equal,
            fill=-2048.0, base=-128 * j + 1, pattern=[[1, 512]],
            channel_multiplier=-1))
        if j > 0:
            gchain(lambda g, j=j: g.memset(RJ[0:1, j, 0:128 * j], -2048.0))
    gchain(lambda g: g.memset(epsc[:], EPS))
    gchain(lambda g: g.memset(VT[:, :, :, 64:66], 0.0))
    tk["init"] = gchain(lambda g: g.memset(VT[:, :, :, 64:65], 1.0))
    INIT = tk["init"]

    tk["bo_ld"] = s.emit("sync", lambda e: e.dma_start(out=bost[:], in_=bo[:]),
                         inc="dxa", inc_by=16)
    tk["gb_ld"] = s.emit("sync", lambda e: e.dma_start(out=gbst[:], in_=gb[:]),
                         inc="dxb", inc_by=16)
    for half in range(2):
        tkp = s.emit("pe", lambda t, half=half: t.matmul(
            pB[:, 0:512], ones1f[0:1, :], bost[0:1, half * 512:(half + 1) * 512],
            start=True, stop=True),
            waits=[tk["bo_ld"], INIT] + last("bo_bc0"), inc="pe")
        tk["bo_bc%d" % half] = s.emit(
            "dve", lambda v, half=half: v.tensor_copy(
                bo_bc[:, half * 512:(half + 1) * 512], pB[:, 0:512]),
            waits=[tkp], inc="dve")
    tkp = s.emit("pe", lambda t: t.matmul(pB[:, 0:8], ones1f[0:1, :], gbst[:],
                                          start=True, stop=True),
                 waits=[tk["gb_ld"], tk["bo_bc0"], tk["bo_bc1"]], inc="pe")
    tk["gb_bc"] = s.emit("dve", lambda v: v.tensor_copy(gb_bc[:], pB[:, 0:8]),
                         waits=[tkp], inc="dve")
    pB_free = tk["gb_bc"]

    def ln_pipeline(par, src_ap, dst_ap, waits, out_name, f32out=False):
        sp = str(par)
        t1 = s.emit("dve", lambda v: v.reduce_sum(mv[:, par, 0:1], src_ap, axis=AX.X),
                    waits=list(waits) + last("ln_mv_rd" + sp), inc="dve")
        tq = s.emit("act", lambda a: a.activation(dst_ap, src_ap, AF.Square,
                                                  accum_out=mv[:, par, 1:2]),
                    waits=list(waits) + last("ln_mv_rdA" + sp), inc="act")
        t2a = s.emit("dve", lambda v: v.tensor_tensor(
            tln[:, par, :], mv[:, par, 0:1], mv[:, par, 0:1], op=ALU.mult),
            waits=[tq], inc="dve")
        t2 = s.emit("dve", lambda v: v.scalar_tensor_tensor(
            bnout[:, par, 0:1], tln[:, par, :], -1.0 / D, mv[:, par, 1:2],
            ALU.mult, ALU.add), inc="dve")
        tk["ln_mv_rdA" + sp] = t2
        t3 = s.emit("act", lambda a: a.activation(tln[:, par, :], bnout[:, par, 0:1],
                                                  AF.Ln, bias=EPS, scale=1.0 / D),
                    waits=[t2, INIT] + last("ln_tln_rd" + sp), inc="act")
        t4 = s.emit("act", lambda a: a.activation(rstd[:, par, :], tln[:, par, :],
                                                  AF.Exp, bias=0.0, scale=-0.5),
                    waits=last("ln_rstd_rd" + sp), inc="act")
        tk["ln_tln_rd" + sp] = t4
        t5 = s.emit("dve", lambda v: v.scalar_tensor_tensor(
            nbv[:, par, :], mv[:, par, 0:1], -1.0 / D, rstd[:, par, :],
            ALU.mult, ALU.mult),
            waits=[t4] + last("ln_nb_rd" + sp), inc="dve")
        tk["ln_mv_rd" + sp] = t5
        tk["ln_rstd_rd" + sp] = t5
        if f32out:
            t6 = s.emit("dve", lambda v: v.tensor_scalar(
                dst_ap, src_ap, rstd[:, par, :], nbv[:, par, :], ALU.mult, ALU.add),
                waits=[t5], inc="dve")
        else:
            t6 = s.emit("act", lambda a: a.activation(
                dst_ap, src_ap, AF.Identity,
                bias=nbv[:, par, :], scale=rstd[:, par, :]),
                waits=[t5], inc="act")
        tk["ln_nb_rd" + sp] = t6
        tk[out_name] = t6
        return t6

    tp_par = [0]

    def transpose_tile(src_tile, dst_ap, waits, split=False):
        par = tp_par[0] % 2
        tp_par[0] += 1
        plan = ((("sync", "tp"), ("act", "tr")) if split else (("sync", "tp"),))
        nh = len(plan)
        for h, (eng, pre) in enumerate(plan):
            sem = "%s%d" % (pre, par)
            t = s.emit(eng, lambda e, h=h, nh=nh: e.dma_start_transpose(
                out=dst_ap[:, h * (8 // nh):(h + 1) * (8 // nh), :],
                in_=src_tile[:, h * (1024 // nh):(h + 1) * (1024 // nh)]),
                waits=list(waits) + [s.tok(sem)], inc=sem, inc_by=16)
            tk["%s_last%d" % (pre, par)] = t
        return last("tp_last0", "tp_last1", "tr_last0", "tr_last1")

    def pe_transpose_tile(src_tile, dst_ap, waits, aa):
        pbk = pOrb[aa]
        tpe = None
        for cc in range(8):
            tpe = s.emit("pe", lambda t, cc=cc, pbk=pbk: t.transpose(
                pbk[:, cc * 128:(cc + 1) * 128],
                src_tile[:, cc * 128:(cc + 1) * 128], ident[:]),
                waits=(list(waits) + ([pO_free[aa]] if pO_free[aa] else []))
                if cc == 0 else [],
                inc="pe" if cc == 7 else None)
        tev = s.emit("dve", lambda v, pbk=pbk: v.tensor_copy(
            dst_ap, pbk[:].rearrange("p (c t) -> p c t", c=8)),
            waits=[tpe], inc="dve")
        pO_free[aa] = tev
        return tev

    LNOb = LNO.bitcast(BF16)
    XNS = None
    XG_free = [tk["bo_ld"], tk["gb_ld"]]
    XN_free = [[], [], [], []]
    pS_free = [None, None, None, None]
    pW_free = None
    pO_free = [None, None]

    for b in range(B):
        wq_tok = s.emit("gp", lambda g: g.dma_start(
            out=WQS[:], in_=wq.rearrange("(cc p) d -> p cc d", p=128)),
            waits=last("A2_done"), inc="swq", inc_by=16)
        wk_tok = s.emit("gp", lambda g: g.dma_start(
            out=WKS[:], in_=wk.rearrange("(cc p) d -> p cc d", p=128)),
            waits=last("A2_done"), inc="swk", inc_by=16)
        wv_tok = s.emit("gp", lambda g: g.dma_start(
            out=WVS[:], in_=wv.rearrange("(cc p) d -> p cc d", p=128)),
            waits=last("A2_done"), inc="swv", inc_by=16)

        xn1t_ready = []
        stat_jobs = []
        ip = 0
        XNS = [XN[:, 0, :], XN[:, 1, :], LNOb[:, 0, 0:1024], LNOb[:, 1, 0:1024]]

        def emit_a_chains(g):
            hoist2 = (b == 1 and g == 0)
            for tt in (range(2, 4) if hoist2 else range(4)):
                tile = 4 * g + tt
                j = tile % 2
                lt = s.emit("sync", lambda e, b=b, tile=tile, j=j: e.dma_start(
                    out=XG[:, j, :], in_=x_in[b, tile * 128:(tile + 1) * 128, :]),
                    waits=[XG_free[j]], inc="dxa" if j == 0 else "dxb", inc_by=16)
                ln_pipeline(j, XG[:, j, :], XNS[tt],
                            [lt] + XN_free[tt] +
                            (last("lno_rd%d" % j) if tt >= 2 else []),
                            "xn%d" % tt)
                XG_free[j] = tk["xn%d" % tt]

        emit_a_chains(0)
        for g in range(NG):
            hoist2 = (b == 1 and g == 0)
            for tt in range(4):
                wtok = ([tk["xnh%d" % (tt % 2)]] if (hoist2 and tt < 2)
                        else [tk["xn%d" % tt]])
                tev = pe_transpose_tile(XNS[tt],
                                        XN1T[:, :, tt * 128:(tt + 1) * 128],
                                        wtok + last("qkv_done"), tt % 2)
                XN_free[tt] = [tev]
                if tt >= 2:
                    tk["lno_rd%d" % (tt % 2)] = tev
            xn1t_ready = [tev]
            if g + 1 < NG:
                emit_a_chains(g + 1)

            for wi, (Wsrc, wtok, DT) in enumerate(
                    ((WQS, wq_tok, QT), (WKS, wk_tok, KT))):
                for oc in range(8):
                    i = ip % 4
                    ip += 1
                    tmm = None
                    for cc in range(8):
                        tmm = s.emit("pe", lambda t, Wsrc=Wsrc, oc=oc, cc=cc, i=i: t.matmul(
                            pS[i], Wsrc[:, cc, oc * 128:(oc + 1) * 128],
                            XN1T[:, cc, :], start=(cc == 0), stop=(cc == 7)),
                            waits=(xn1t_ready + [wtok] +
                                   ([pS_free[i]] if pS_free[i] else [])) if cc == 0 else [],
                            inc="pe" if cc == 7 else None)
                    tev = s.emit("dve", lambda v, DT=DT, oc=oc, g=g, i=i: v.tensor_copy(
                        DT[:, oc, g * 512:(g + 1) * 512], pS[i]),
                        waits=[tmm], inc="dve")
                    pS_free[i] = tev
                    tsq = s.emit("gp", lambda p, DT=DT, oc=oc, g=g: p.tensor_tensor(
                        sqb[:, oc, :], DT[:, oc, g * 512:(g + 1) * 512],
                        DT[:, oc, g * 512:(g + 1) * 512], op=ALU.mult),
                        waits=[tev] + last("sqb_rd%d" % oc), inc="gp")
                    stat_jobs.append((wi, oc, g, tsq))

            tev = None
            vidx = 0
            for tt in range(4):
                kc = 4 * g + tt
                for dh in range(2):
                    i = ip % 4
                    ip += 1
                    tmm = None
                    for cc in range(8):
                        tmm = s.emit("pe", lambda t, tt=tt, dh=dh, cc=cc, i=i: t.matmul(
                            pS[i], XN1T[:, cc, tt * 128:(tt + 1) * 128],
                            WVS[:, cc, dh * 512:(dh + 1) * 512],
                            start=(cc == 0), stop=(cc == 7)),
                            waits=(xn1t_ready + [wv_tok] +
                                   ([pS_free[i]] if pS_free[i] else [])) if cc == 0 else [],
                            inc="pe" if cc == 7 else None)
                    tev = s.emit("dve", lambda v, kc=kc, dh=dh, i=i: v.tensor_copy(
                        VT[:, kc, dh * 8:(dh + 1) * 8, 0:64],
                        pS[i].rearrange("p (h d) -> p h d", d=64)),
                        waits=[tmm, INIT], inc="dve")
                    pS_free[i] = tev
                    for sidx in (2 * vidx, 2 * vidx + 1):
                        if sidx >= len(stat_jobs):
                            continue
                        wi, oc, gs, tsq = stat_jobs[sidx]
                        pbank = pB if sidx % 2 == 0 else pW
                        tpp = s.emit("pe", lambda t, oc=oc, pbank=pbank: t.matmul(
                            pbank[0:1, :], onescol[:], sqb[:, oc, :],
                            start=True, stop=True),
                            waits=[tsq, INIT, pB_free if sidx % 2 == 0 else pW_free],
                            inc="pe")
                        tk["sqb_rd%d" % oc] = tpp
                        col = wi * 32 + gs * 8 + oc
                        trd = s.emit("dve", lambda v, col=col, pbank=pbank: v.tensor_reduce(
                            statg[0:1, col:col + 1], pbank[0:1, :], AX.X, ALU.max),
                            waits=[tpp], inc="dve")
                        if sidx % 2 == 0:
                            pB_free = trd
                        else:
                            pW_free = trd
                    vidx += 1
            tk["qkv_done"] = tev
            stat_jobs = []

        a1_done = tk["qkv_done"]
        sg = statg.rearrange("a (w g c) -> a w c g", w=2, c=8)
        for ci in range(8):
            s.emit("dve", lambda v, ci=ci: v.tensor_reduce(
                qqm[0:1, ci:ci + 1], sg[0:1, 0, ci, :], AX.X, ALU.max),
                waits=[a1_done], inc="dve")
            s.emit("dve", lambda v, ci=ci: v.tensor_reduce(
                kkm[0:1, ci:ci + 1], sg[0:1, 1, ci, :], AX.X, ALU.max), inc="dve")
        tqa = s.emit("dve", lambda v: v.tensor_tensor(
            kkm[0:1, 8:16], qqm[0:1, 0:8], kkm[0:1, 0:8], op=ALU.add),
            waits=last("tmp11b_rd"), inc="dve")
        tb = s.emit("pe", lambda t: t.matmul(
            pB[:, 0:8], ones1f[0:1, :], kkm[0:1, 8:16], start=True, stop=True),
            waits=[tqa, pB_free], inc="pe")
        tk["tmp11b_rd"] = tb
        tmc = s.emit("act", lambda a: a.activation(
            MC[:, 0:8, 0], pB[:, 0:8], AF.Identity, bias=0.0, scale=-BETA / 2),
            waits=[tb] + last(*["mc_rd%d" % c for c in range(8)]), inc="act")
        for ci in range(8):
            tk["mc_%d" % ci] = tmc
        pB_free = tmc
        if dump_qkv and b == 0:
            s.emit("gp", lambda g: g.dma_start(out=dbgq[:], in_=QT[:]),
                   waits=[a1_done], inc="dbg", inc_by=16)
            s.emit("gp", lambda g: g.dma_start(out=dbgk[:], in_=KT[:]),
                   waits=[a1_done], inc="dbg", inc_by=16)
            s.emit("gp", lambda g: g.dma_start(out=dbgv[:], in_=VT[:]),
                   waits=[a1_done], inc="dbg", inc_by=16)
            s.emit("gp", lambda g: g.dma_start(out=dbgx[:], in_=XN1T[:]),
                   waits=[a1_done], inc="dbg", inc_by=16)
            s.emit("gp", lambda g: g.dma_start(out=dbgmv[:], in_=mv[:]),
                   waits=[a1_done], inc="dbg", inc_by=16)
            s.emit("gp", lambda g: g.dma_start(out=dbgrs[:], in_=rstd[:]),
                   waits=[a1_done], inc="dbg", inc_by=16)

        if b == 0:
            for tt in range(2):
                j = tt % 2
                lt = s.emit("gp", lambda e, tile=tt, j=j: e.dma_start(
                    out=XA[:, j, :], in_=x_in[1, tile * 128:(tile + 1) * 128, :]),
                    waits=[], inc="hxa" if j == 0 else "hxb", inc_by=16)
                ln_pipeline(2 + j, XA[:, j, :], XN[:, j, :],
                            [lt] + XN_free[j], "xnh%d" % j)

        wo_tok = s.emit("gp", lambda g: g.dma_start(
            out=WOS[:], in_=wo.rearrange("(cc p) c -> p cc c", p=128)),
            waits=[a1_done], inc="swo", inc_by=16)

        DEPTH = 2
        jobs = []
        for g in range(NG):
            for ci in range(8):
                nkc = 4 * g + 4
                for kc in range(nkc):
                    jobs.append((g, ci, kc, nkc))
        PT_tok = [None] * len(jobs)
        PT_free = [None, None, None]
        ogu_free = [None] * 6
        ognorm_last = None
        nj = len(jobs)
        pr_idx = [0]

        def emit_S(j):
            nonlocal pB_free
            g, ci, kc, nkc = jobs[j]
            i2 = j % 2
            i4 = j % 3
            diag = kc >= 4 * g
            jm = kc - 4 * g
            tS = None
            for aa in range(2):
                hb = aa * 64
                tS = s.emit("pe", lambda t, hb=hb, ci=ci, g=g, kc=kc, i2=i2, aa=aa, diag=diag: t.matmul(
                    pS[2 * i2 + aa], KT[hb:hb + 64, ci, kc * 128:(kc + 1) * 128],
                    QT[hb:hb + 64, ci, g * 512:(g + 1) * 512],
                    start=True, stop=not diag, tile_position=(hb, 0)),
                    waits=([a1_done] + ([pS_free[i2]] if pS_free[i2] else []))
                    if aa == 0 else [],
                    inc=None if diag else ("pe" if aa == 1 else None))
            if diag:
                for aa in range(2):
                    tS = s.emit("pe", lambda t, i2=i2, aa=aa, jm=jm: t.matmul(
                        pS[2 * i2 + aa], LBT[:], RJ[:, jm, :],
                        start=False, stop=True),
                        waits=[INIT] if aa == 0 else [],
                        inc="pe" if aa == 1 else None)
            tE = s.emit("act", lambda a, i2=i2, i4=i4, ci=ci: a.activation(
                PTB[:, 2 * i4:2 * i4 + 2, :], pSpair[i2][:], AF.Exp,
                bias=MC[:, ci, :], scale=BETA),
                waits=[tS, tk["mc_%d" % ci]] + ([PT_free[i4]] if PT_free[i4] else []),
                inc="act")
            pS_free[i2] = tE
            tk["mc_rd%d" % ci] = tE
            PT_tok[j] = tE

        norm_q = []
        wo_q = []

        def advance():
            nonlocal pB_free, pW_free, ognorm_last
            if wo_q:
                e = wo_q[0]
                if e["st"] == 0:
                    emit_wo_mm(e)
                else:
                    emit_wo_epi(e)
                    wo_q.pop(0)
                    return
                e["st"] += 1
                return
            if norm_q:
                e = norm_q[0]
                st = e["st"]
                g, ci, rot = e["g"], e["ci"], e["rot"]
                sl0, sl1 = 2 * (rot % 3), 2 * (rot % 3) + 1
                l0, l1 = 2 * (rot % 2), 2 * (rot % 2) + 1
                if st == 0:
                    t_ = s.emit("act", lambda a, sl0=sl0, l0=l0: a.activation(
                        LNV[64:65, l0:l0 + 2, :], OGU[64:65, sl0:sl0 + 2, :],
                        AF.Ln),
                        waits=e["ev"] + last("lnv_rd%d" % l0, "lnv_rd%d" % l1),
                        inc="act")
                    e["ln"] = s.emit("act", lambda a, l0=l0: a.activation(
                        LNV[64:65, l0:l0 + 2, :], LNV[64:65, l0:l0 + 2, :],
                        AF.Exp, bias=0.0, scale=-1.0),
                        waits=[t_], inc="act")
                elif st == 1:
                    p = rot % 2
                    t_ = s.emit("sync", lambda ee, p=p, l0=l0: ee.dma_start(
                        out=lscr[p], in_=LNV[64:65, l0:l0 + 2, :]),
                        waits=[e["ln"]] + last("lscr_rd%d" % p),
                        inc="lh%d" % p, inc_by=16)
                    tk["lnv_rd%d" % l0] = t_
                    tk["lnv_rd%d" % l1] = t_
                    tk["lh_last%d" % p] = t_
                    e["h1"] = t_
                elif st == 2:
                    p = rot % 2
                    t_ = s.emit("sync", lambda ee, p=p, l0=l0: ee.dma_start(
                        out=SLB[0:64, l0:l0 + 2, :],
                        in_=lscr[p].broadcast_to((64, 2, 512))),
                        waits=[e["h1"]] + last("slb_rd%d" % l0, "slb_rd%d" % l1),
                        inc="lb%d" % p, inc_by=16)
                    tk["lscr_rd%d" % p] = t_
                    tk["lb_last%d" % p] = t_
                    e["sx"] = [t_, t_]
                elif st == 3:
                    tN = s.emit("gp", lambda v, ci=ci, sl0=sl0, l0=l0: v.tensor_tensor(
                        OGT[0:64, ci, :], OGU[0:64, sl0, :], SLB[0:64, l0, :],
                        op=ALU.mult), waits=[e["sx"][0]] + last("ogt_rd"), inc="gp")
                    tk["slb_rd%d" % l0] = tN
                    ogu_free[sl0] = tN
                    pp = rot % 2
                    tN2 = s.emit("gp", lambda v, sl1=sl1, l1=l1, pp=pp: v.tensor_tensor(
                        OGTtmp[0:64, pp, :], OGU[0:64, sl1, :], SLB[0:64, l1, :],
                        op=ALU.mult),
                        waits=[e["sx"][1]] + last("ogtmp_rd%d" % pp, "ogt_rd"), inc="gp")
                    tk["slb_rd%d" % l1] = tN2
                    tN2 = s.emit("gp", lambda ee, ci=ci, pp=pp: ee.dma_start(
                        out=OGT[64:128, ci, :], in_=OGTtmp[0:64, pp, :]),
                        waits=[tN2], inc="som%d" % pp, inc_by=16)
                    tk["ogtmp_rd%d" % pp] = tN2
                    tk["som_last%d" % pp] = tN2
                    ogu_free[sl1] = tN2
                    ognorm_last = tN
                    norm_q.pop(0)
                    if ci == 7:
                        for qt in range(4):
                            wo_q.append({"g": g, "qt": qt, "st": 0})
                    return
                e["st"] += 1
                return
        pr_idx = [0]

        def emit_V(j):
            nonlocal pB_free, ognorm_last
            g, ci, kc, nkc = jobs[j]
            i4 = j % 3
            advance()
            tV = None
            for aa in range(2):
                tV = s.emit("pe", lambda t, ci=ci, kc=kc, i4=i4, nkc=nkc, aa=aa: t.matmul(
                    pOr[aa][0:65, :], VT[:, kc, 2 * ci + aa, 0:65],
                    PTB[:, 2 * i4 + aa, :],
                    start=(kc == 0), stop=(kc == nkc - 1)),
                    waits=[PT_tok[j]] +
                    ([pO_free[aa]] if (kc == 0 and pO_free[aa]) else []),
                    inc="pe")
            PT_free[i4] = tV
            if kc != nkc - 1:
                return
            rot = pr_idx[0]
            pr_idx[0] += 1
            ev = []
            for aa in range(2):
                sl = 2 * (rot % 3) + aa
                tEv = s.emit("dve", lambda v, aa=aa, sl=sl: v.tensor_copy(
                    OGU[0:65, sl, :], pOr[aa][0:65, :]),
                    waits=[tV] + ([ogu_free[sl]] if ogu_free[sl] else []), inc="dve")
                pO_free[aa] = tEv
                ev.append(tEv)
            norm_q.append({"g": g, "ci": ci, "rot": rot, "st": 0, "ev": ev})

        def emit_wo_mm(e):
            nonlocal pW_free, pB_free
            g, qt = e["g"], e["qt"]
            tile = g * 4 + qt
            j = tile % 2
            e["lt"] = s.emit("sync", lambda ee, b=b, tile=tile, j=j: ee.dma_start(
                out=XG[:, j, :], in_=x_in[b, tile * 128:(tile + 1) * 128, :]),
                waits=[XG_free[j]], inc="dxa" if j == 0 else "dxb", inc_by=16)
            ty = []
            for half in range(2):
                tmm = None
                for ih in range(8):
                    tmm = s.emit("pe", lambda t, ih=ih, qt=qt, half=half: t.matmul(
                        pW[:],
                        OGT[:, ih, qt * 128:(qt + 1) * 128],
                        WOS[:, ih, half * 512:(half + 1) * 512],
                        start=(ih == 0), stop=(ih == 7)),
                        waits=([ognorm_last, wo_tok] +
                               last("som_last0", "som_last1") +
                               ([pW_free] if pW_free else []))
                        if ih == 0 else [],
                        inc="pe" if ih == 7 else None)
                if qt == 3 and half == 1:
                    tk["ogt_rd"] = tmm
                tyh = s.emit("dve", lambda v, j=j, half=half: v.scalar_tensor_tensor(
                    YST[:, j, half * 512:(half + 1) * 512], pW[:], 1.0,
                    bo_bc[:, half * 512:(half + 1) * 512], ALU.mult, ALU.add),
                    waits=[tmm] + (last("yst_rd%d" % j) if half == 0 else []) +
                    [tk["bo_bc0"], tk["bo_bc1"]], inc="dve")
                pW_free = tyh
                ty.append(tyh)
            e["ty"] = ty

        def emit_wo_epi(e):
            nonlocal pB_free
            g, qt = e["g"], e["qt"]
            tile = g * 4 + qt
            j = tile % 2
            ln_pipeline(j, YST[:, j, :], LNO[:, j, :],
                        [e["ty"][1]] + last("lno_rd%d" % j), "ln_y", f32out=True)
            tx2 = s.emit("dve", lambda v, j=j: v.tensor_add(
                YST[:, j, :], LNO[:, j, :], XG[:, j, :]),
                waits=[tk["ln_y"], e["lt"]], inc="dve")
            tk["lno_rd%d" % j] = tx2
            XG_free[j] = tx2
            tst = s.emit("sync", lambda ee, b=b, tile=tile, j=j: ee.dma_start(
                out=x2d[b, tile * 128:(tile + 1) * 128, :], in_=YST[:, j, :]),
                waits=[tx2], inc="dsa" if j == 0 else "dsb", inc_by=16)
            tk["yst_rd%d" % j] = tst

        for j in range(nj):
            emit_S(j)
            if j >= DEPTH:
                emit_V(j - DEPTH)
        for j in range(nj - DEPTH, nj):
            emit_V(j)
        while norm_q or wo_q:
            advance()

        if dump_qkv and b == 0:
            s.emit("gp", lambda g: g.dma_start(out=dbgo[:], in_=OGT[:]),
                   waits=[ognorm_last], inc="dbg", inc_by=16)
        tk["A2_done"] = s.tok("dve")

    att_done = [s.tok("pe"), s.tok("act"), s.tok("dve"), s.tok("gp"),
                s.tok("dsa"), s.tok("dsb")]
    stores_done = [s.tok("dsa"), s.tok("dsb")]
    w2buf_free = [None, None]
    wg_tok = s.emit("gp", lambda g: g.dma_start(
        out=WGS[:], in_=wg.rearrange("(cc p) e -> p cc e", p=128)),
        waits=att_done, inc="swg", inc_by=16)
    wso_tok = s.emit("gp", lambda g: g.dma_start(
        out=WSOS[:], in_=wso.rearrange("(hc p) c -> p hc c", p=128)),
        waits=att_done, inc="swso", inc_by=16)
    wsi_tok = [s.emit("gp", lambda q: q.dma_start(
        out=WSH[:], in_=wsi.rearrange("(cc p) f -> p cc f", p=128)),
        waits=att_done, inc="swsh", inc_by=16)]
    w1_tok = [s.emit("gp", lambda q: q.dma_start(
        out=W1S[:], in_=w1.rearrange("(cc p) f -> p cc f", p=128)),
        waits=att_done, inc="swsh", inc_by=16)]
    tk["gsc8z"] = s.emit("gp", lambda g: g.memset(gsc8[:], 0.0),
                         waits=att_done, inc="gp")
    tk["gsc8_rd"] = tk["gsc8z"]

    def emit_chains01(b, hf):
        for tt in range(2):
            tile = 8 * hf + tt
            j = tile % 2
            lt = s.emit("sync", lambda e, b=b, tile=tile, j=j: e.dma_start(
                out=XG[:, j, :], in_=x2d[b, tile * 128:(tile + 1) * 128, :]),
                waits=[XG_free[j]] + (stores_done if tile == 0 else []),
                inc="dxa" if j == 0 else "dxb", inc_by=16)
            ln_pipeline(2 + j, XG[:, j, :], XN[:, j, :],
                        [lt] + XN_free[tt], "xn2_%d" % tt)
            XG_free[j] = tk["xn2_%d" % tt]

    def emit_tiles(b, hf, skip01=False):
        bar = att_done if b == 0 else []
        XNSF = [XN[:, 0, :], XN[:, 1, :],
                HST[:, 0:2, :].rearrange("p a b -> p (a b)"),
                HST[:, 2:4, :].rearrange("p a b -> p (a b)")]
        xn2rg = []
        for g0 in range(2):
            g = 2 * hf + g0
            for tt in range(4):
                if skip01 and g0 == 0 and tt < 2:
                    continue
                tile = 4 * g + tt
                j = tile % 2
                lt = s.emit("sync", lambda e, b=b, tile=tile, j=j: e.dma_start(
                    out=XG[:, j, :], in_=x2d[b, tile * 128:(tile + 1) * 128, :]),
                    waits=[XG_free[j]] + (stores_done if tile == 0 else []),
                    inc="dxa" if j == 0 else "dxb", inc_by=16)
                ln_pipeline(2 + j, XG[:, j, :], XNSF[tt],
                            [lt] + XN_free[tt] +
                            (last("hst_rd") + (bar if tile < 8 else [])
                             if tt >= 2 else []), "xn2_%d" % tt)
                XG_free[j] = tk["xn2_%d" % tt]
            te = None
            for tt in range(4):
                tile = 4 * g + tt
                tl = tile - 8 * hf
                w_ = ([tk["xn2_%d" % tt]] + (bar if tile == 0 else []) +
                      (last("xn2t_rd") if tl == 0 else []))
                tev = pe_transpose_tile(
                    XNSF[tt], XN2T[:, :, tl * 128:(tl + 1) * 128],
                    w_, tt % 2)
                te = [tev]
                XN_free[tt] = te
                if tt >= 2:
                    tk["hst_rd"] = tev
            xn2rg.append(te)
        return xn2rg

    halves = [(bb, hh) for bb in range(B) for hh in range(2)]
    xn2pend = {halves[0]: emit_tiles(*halves[0])}
    ht_ready = None
    for idx, (b, hf) in enumerate(halves):
        bar = att_done if b == 0 else []
        xn2rg = xn2pend.pop((b, hf))
        if idx + 1 < len(halves):
            emit_chains01(*halves[idx + 1])

        def swiglu_group(wtok, wsrc, g0, g, dst_fn, war_name):
            th = None
            for fc in range(8):
                ia = sw_ip[0] % 4
                ib_ = (sw_ip[0] + 1) % 4
                sw_ip[0] += 2
                ta = None
                for cc in range(8):
                    ta = s.emit("pe", lambda t, fc=fc, cc=cc, g0=g0, ia=ia, wsrc=wsrc: t.matmul(
                        pS[ia], wsrc[:, cc, fc * 128:(fc + 1) * 128],
                        XN2T[:, cc, g0 * 512:(g0 + 1) * 512],
                        start=(cc == 0), stop=(cc == 7)),
                        waits=(wtok + xn2_ready + ([pS_free[ia]] if pS_free[ia] else []))
                        if cc == 0 else [],
                        inc="pe" if cc == 7 else None)
                tb = None
                for cc in range(8):
                    tb = s.emit("pe", lambda t, fc=fc, cc=cc, g0=g0, ib_=ib_, wsrc=wsrc: t.matmul(
                        pS[ib_], wsrc[:, cc, 1024 + fc * 128:1024 + (fc + 1) * 128],
                        XN2T[:, cc, g0 * 512:(g0 + 1) * 512],
                        start=(cc == 0), stop=(cc == 7)),
                        waits=([pS_free[ib_]] if pS_free[ib_] else [])
                        if cc == 0 else [],
                        inc="pe" if cc == 7 else None)
                si = fc % 2
                tsil = s.emit("act", lambda a, ia=ia, si=si: a.activation(
                    SILA[:, si, :], pS[ia], AF.Silu),
                    waits=[ta] + ([sila_free[si]] if sila_free[si] else []), inc="act")
                pS_free[ia] = tsil
                th = s.emit("dve", lambda v, fc=fc, g=g, si=si, ib_=ib_: v.tensor_tensor(
                    dst_fn(fc, g), SILA[:, si, :], pS[ib_], op=ALU.mult),
                    waits=[tb, tsil] + (last(war_name) if fc == 0 else []), inc="dve")
                sila_free[si] = th
                pS_free[ib_] = th
            return th

        sila_free = [None, None]
        sw_ip = [0]
        for g0 in range(2):
            g = 2 * hf + g0
            xn2_ready = xn2rg[g0]

            tgm = None
            for cc in range(8):
                tgm = s.emit("pe", lambda t, cc=cc, g0=g0: t.matmul(
                    pB[0:8, :], WGS[:, cc, :], XN2T[:, cc, g0 * 512:(g0 + 1) * 512],
                    start=(cc == 0), stop=(cc == 7)),
                    waits=(xn2_ready + [wg_tok, pB_free]) if cc == 0 else [],
                    inc="pe" if cc == 7 else None)
            tge = s.emit("dve", lambda v: v.tensor_copy(gsc8[0:8, :], pB[0:8, :]),
                         waits=[tgm] + last("gsc8_rd"), inc="dve")
            pB_free = tge
            pgx = g % 2
            tgx = s.emit("sync", lambda e: e.dma_start_transpose(
                out=glt4p[:], in_=gsc8[:]),
                waits=[tge] + last("glt4p_rd", "gsc8z") + [s.tok("gx%d" % pgx)],
                inc="gx%d" % pgx, inc_by=16)
            tk["gsc8_rd"] = tgx
            tk["gx_last%d" % pgx] = tgx
            tgl = s.emit("dve", lambda v: v.tensor_tensor(
                glt4[:], glt4p[:, :, 0:NE],
                gb_bc[:].rearrange("p (x e) -> p x e", x=1).broadcast_to([128, 4, NE]),
                op=ALU.add),
                waits=[tgx, tk["gb_bc"]] + last("glt_rd"), inc="dve")
            tk["glt4p_rd"] = tgl
            t1 = s.emit("dve", lambda v: v.reduce_max(
                gm14[:], glt4[:], axis=AX.X), waits=[tgl], inc="dve")
            tx = s.emit("act", lambda a: a.activation(gex4[:], glt4[:], AF.Exp),
                        waits=[tgl] + last("gex_rd"), inc="act")
            ts_ = s.emit("dve", lambda v: v.reduce_sum(
                gs4[:], gex4[:], axis=AX.X), waits=[tx], inc="dve")
            t3 = s.emit("dve", lambda v: v.tensor_tensor(
                gk4[:], glt4[:], gm14[:].broadcast_to([128, 4, 8]),
                op=ALU.is_ge), waits=[t1], inc="dve")
            t4 = s.emit("dve", lambda v: v.scalar_tensor_tensor(
                gk4[:], gk4[:], -1e30, glt4[:], ALU.mult, ALU.add), inc="dve")
            t5 = s.emit("dve", lambda v: v.reduce_max(
                gm14[:], gk4[:], axis=AX.X), inc="dve")
            t6 = s.emit("dve", lambda v: v.tensor_tensor(
                gk4[:], glt4[:], gm14[:].broadcast_to([128, 4, 8]), op=ALU.is_ge), inc="dve")
            tk["glt_rd"] = t6
            t7 = s.emit("dve", lambda v: v.tensor_tensor(
                gk4[:], gk4[:], gex4[:], op=ALU.mult), inc="dve")
            tk["gex_rd"] = t7
            t8 = s.emit("dve", lambda v: v.reciprocal(grs4[:], gs4[:]),
                        waits=[ts_], inc="dve")
            t9 = s.emit("dve", lambda v, g=g: v.tensor_tensor(
                WD[:, g * 4:(g + 1) * 4, :], gk4[:], grs4[:].broadcast_to([128, 4, 8]),
                op=ALU.mult), inc="dve")
            for qt in range(4):
                tk["wd_%d" % (g * 4 + qt)] = t9
            tk["gsc8_rd"] = t9

            th = swiglu_group(wsi_tok, WSH, g0, g,
                              lambda fc, g: HST[:, fc, :], "hst_rd")
            for qt in range(4):
                tile = g * 4 + qt
                tloc = tile - 8 * hf
                trc = None
                for half in range(2):
                    tmm = None
                    for hc in range(8):
                        tmm = s.emit("pe", lambda t, hc=hc, qt=qt, half=half: t.matmul(
                            pW[:],
                            HST[:, hc, qt * 128:(qt + 1) * 128],
                            WSOS[:, hc, half * 512:(half + 1) * 512],
                            start=(hc == 0), stop=(hc == 7)),
                            waits=([th, wso_tok] +
                                   ([pW_free] if pW_free else []))
                            if hc == 0 else [],
                            inc="pe" if hc == 7 else None)
                    trc = s.emit("dve", lambda v, tloc=tloc, half=half: v.tensor_copy(
                        RACC[:, tloc, half * 512:(half + 1) * 512], pW[:]),
                        waits=[tmm] + (bar if (tile == 0 and half == 0) else []),
                        inc="dve")
                    pW_free = trc
                tk["racc_%d" % tile] = trc
            tk["hst_rd"] = tmm

        for g0 in range(2):
            g = 2 * hf + g0
            xn2_ready = xn2rg[g0]
            ht_ready = swiglu_group(
                w1_tok, W1S, g0, g,
                lambda fc, g: HT[:, fc, (g % 2) * 512:((g % 2) + 1) * 512],
                "ht_rd")
        tk["xn2t_rd"] = s.tok("pe")

        if idx + 1 < len(halves):
            xn2pend[halves[idx + 1]] = emit_tiles(*halves[idx + 1],
                                                  skip01=True)

        for e in range(NE):
            for ch in range(2):
                ib = (e * 2 + ch) % 2
                wt = s.emit("gp", lambda g, e=e, ch=ch, ib=ib: g.dma_start(
                    out=W2S[:, ib, :, :],
                    in_=w2[e].rearrange("(hc p) c -> p hc c", p=128)[:, :, ch * 512:(ch + 1) * 512]),
                    waits=[w2buf_free[ib]] if w2buf_free[ib] else att_done,
                    inc="sw2a" if ib == 0 else "sw2b", inc_by=16)
                for qt in range(8):
                    tile = hf * 8 + qt
                    i2 = qt % 4
                    tmm = None
                    for hc in range(8):
                        tmm = s.emit("pe", lambda t, hc=hc, qt=qt, ib=ib, i2=i2: t.matmul(
                            pS[i2], HT[:, hc, qt * 128:(qt + 1) * 128],
                            W2S[:, ib, hc, :], start=(hc == 0), stop=(hc == 7)),
                            waits=([wt, ht_ready] +
                                   ([pS_free[i2]] if pS_free[i2] else []))
                            if hc == 0 else [],
                            inc="pe" if hc == 7 else None)
                    tup = s.emit("dve",
                                 lambda v, e=e, qt=qt, tile=tile, ch=ch, i2=i2: v.scalar_tensor_tensor(
                                     RACC[:, qt, ch * 512:(ch + 1) * 512], pS[i2],
                                     WD[:, tile, e:e + 1],
                                     RACC[:, qt, ch * 512:(ch + 1) * 512],
                                     ALU.mult, ALU.add),
                                 waits=[tmm, tk["wd_%d" % tile], tk["racc_%d" % tile]],
                                 inc="dve")
                    tk["racc_%d" % tile] = tup
                    pS_free[i2] = tup
                    if e == NE - 1 and ch == 1:
                        j = tile % 2
                        lt = s.emit("gp", lambda ee, b=b, tile=tile, j=j: ee.dma_start(
                            out=YST[:, j, :], in_=x2d[b, tile * 128:(tile + 1) * 128, :]),
                            waits=last("yst_rd%d" % j), inc="fxa" if j == 0 else "fxb",
                            inc_by=16)
                        ln_pipeline(j, RACC[:, qt, :], LNO[:, j, :],
                                    [tup] + last("lno_rd%d" % j), "ln_f", f32out=True)
                        tof = s.emit("dve", lambda v, j=j: v.tensor_add(
                            YST[:, j, :], LNO[:, j, :], YST[:, j, :]),
                            waits=[tk["ln_f"], lt], inc="dve")
                        tk["lno_rd%d" % j] = tof
                        tst = s.emit("gp", lambda ee, b=b, tile=tile, j=j: ee.dma_start(
                            out=out[b, tile * 128:(tile + 1) * 128, :], in_=YST[:, j, :]),
                            waits=[tof], inc="fsa" if j == 0 else "fsb", inc_by=16)
                        tk["yst_rd%d" % j] = tst
                w2buf_free[ib] = tmm
        tk["ht_rd"] = tmm

    tk["f2_pe"] = s.tok("pe")
    tk["F3_done"] = s.tok("fsa")
    tk["F3_doneb"] = s.tok("fsb")

    s.emit("sync", lambda e: e.nop(),
           waits=[s.tok("dsa"), s.tok("dsb"), s.tok("dbg"),
                  s.tok("tp0"), s.tok("tp1"), s.tok("som0"), s.tok("som1"),
                  s.tok("lh0"), s.tok("lh1"), s.tok("lb0"), s.tok("lb1"),
                  s.tok("fxa"), s.tok("fxb"), s.tok("fsa"), s.tok("fsb"),
                  s.tok("gx0"), s.tok("gx1"), s.tok("hxa"), s.tok("hxb"),
                  s.tok("tr0"), s.tok("tr1"),
                  s.tok("swsh2"), s.tok("swsh3")])

    from contextlib import ExitStack
    with ExitStack() as ctx:
        sems = {}
        for name in ("pe", "act", "dve", "gp", "swq", "swk", "swv", "swo",
                     "swg", "swsh", "swsh2", "swsh3", "swso", "sw2a", "sw2b",
                     "dxa", "dxb", "dsa", "dsb", "dbg",
                     "tp0", "tp1", "som0", "som1",
                     "lh0", "lh1", "lb0", "lb1",
                     "fxa", "fxb", "fsa", "fsb", "gx0", "gx1",
                     "hxa", "hxb", "tr0", "tr1"):
            sems[name] = ctx.enter_context(nc.semaphore(name))
        blk = ctx.enter_context(nc.Block())
        s.replay(blk, sems)
    return nc


def kernel(L0_attn_x, L0_attn_Wq, L0_attn_Wk, L0_attn_Wv, L0_attn_Wo, L0_attn_bo,
           L0_ffn_w_shared_in, L0_ffn_w_shared_out, L0_ffn_w1_shared, L0_ffn_w2,
           L0_ffn_w_gate, L0_ffn_gate_bias):
    x = np.ascontiguousarray(np.asarray(L0_attn_x, dtype=np.float32))
    weights = dict(
        wq=np.ascontiguousarray(np.asarray(L0_attn_Wq, np.float32)),
        wk=np.ascontiguousarray(np.asarray(L0_attn_Wk, np.float32)),
        wv=np.ascontiguousarray(np.asarray(L0_attn_Wv, np.float32)),
        wo=np.ascontiguousarray(np.asarray(L0_attn_Wo, np.float32)),
        bo=np.ascontiguousarray(np.asarray(L0_attn_bo, np.float32).reshape(1, D)),
        wsi=np.ascontiguousarray(np.asarray(L0_ffn_w_shared_in, np.float32)),
        wso=np.ascontiguousarray(np.asarray(L0_ffn_w_shared_out, np.float32)),
        w1=np.ascontiguousarray(np.asarray(L0_ffn_w1_shared, np.float32)),
        w2=np.ascontiguousarray(np.asarray(L0_ffn_w2, np.float32)),
        wg=np.ascontiguousarray(np.asarray(L0_ffn_w_gate, np.float32)),
        gb=np.ascontiguousarray(np.asarray(L0_ffn_gate_bias, np.float32).reshape(1, NE)),
    )
    nc = build_program()
    in_maps = []
    for i in range(NCORES):
        m = dict(weights)
        m["x_in"] = np.ascontiguousarray(x[2 * i:2 * i + 2])
        in_maps.append(m)
    res = run_bass_kernel_spmd(nc, in_maps, core_ids=list(range(NCORES)))
    return np.concatenate([r["out"] for r in res.results], axis=0)
